# version 12
# baseline (speedup 1.0000x reference)
import numpy as np
from contextlib import ExitStack

import concourse.bass as bass
import concourse.mybir as mybir
from concourse.bass_utils import run_bass_kernel_spmd

F32 = mybir.dt.float32
BF16 = mybir.dt.bfloat16
AF = mybir.ActivationFunctionType
ALU = mybir.AluOpType
AX = mybir.AxisListType.X

D = 2048
LK = 8192
TOK = 2048
H = 16
NCK = 16
HFF = 5632
JCK = 44
EPS_LN = 1e-5
EPS_Q = 1e-8

SCALE = float(np.float32(1.0) / np.float32(np.sqrt(np.float32(128.0))))
RS = float(np.sqrt(SCALE))

_KGS = 8
_DO_V = True
_DO_NORM = True
_FF_PARTS = 3
_FF_TH = 2
_NORM_STORES = True
_NORM_COMPUTE = True
_NORM_BCAST = True
_NORM_TARGET_OUT = True
_SKIP_MID = False


class C:
    def __init__(self, h):
        self.h = h
        self.n = 0

    def inc(self, instr, k=1):
        instr.then_inc(self.h, k)
        self.n += k


class DmaRing:

    def __init__(self, nc, es, sems, name, n=2, k=1):
        self.c = [C(es.enter_context(nc.semaphore(f"{name}{i}")))
                  for i in range(n)]
        for cc in self.c:
            sems.append(cc)
        self.n = n
        self.k = k

    def issue(self, e, slot, out, in_, **kw):
        cc = self.c[slot % self.n]
        e.dma_start(out=out, in_=in_, **kw).then_inc(cc.h, 16)
        cc.n += 16

    def wait(self, e, slot):
        e.wait_ge(self.c[slot % self.n].h, (slot // self.n + 1) * self.k * 16)


def build(upto=7, debug=False):
    nc = bass.Bass("TRN2", target_bir_lowering=False)
    sck = "ExternalOutput" if debug else "Internal"

    xb = nc.dram_tensor("xb", [NCK, 128, LK], F32, kind="ExternalInput").ap()
    xm = nc.dram_tensor("xm", [NCK, 128, TOK], F32, kind="ExternalInput").ap()
    wqkv = nc.dram_tensor("wqkv", [D, 3 * D], F32, kind="ExternalInput").ap()
    bq_col = nc.dram_tensor("bq_col", [128, 48], F32, kind="ExternalInput").ap()
    bq_row = nc.dram_tensor("bq_row", [1, 3 * D], F32, kind="ExternalInput").ap()
    wproj = nc.dram_tensor("wproj", [D, D], F32, kind="ExternalInput").ap()
    bp_col = nc.dram_tensor("bp_col", [128, 16], F32, kind="ExternalInput").ap()
    w1 = nc.dram_tensor("w1", [HFF, D], F32, kind="ExternalInput").ap()
    w2 = nc.dram_tensor("w2", [HFF, D], F32, kind="ExternalInput").ap()
    w3 = nc.dram_tensor("w3", [D, HFF], F32, kind="ExternalInput").ap()
    out = nc.dram_tensor("out", [NCK, 128, TOK], F32, kind="ExternalOutput").ap()

    kscr = nc.dram_tensor("kscr", [H, 128, LK], BF16, kind=sck).ap()
    vscr = nc.dram_tensor("vscr", [H, LK, 128], BF16, kind=sck).ap()
    w1q = nc.dram_tensor("w1qs", [HFF, D], BF16, kind=sck).ap()
    w2q = nc.dram_tensor("w2qs", [HFF, D], BF16, kind=sck).ap()
    w3q = nc.dram_tensor("w3qs", [D, HFF], BF16, kind=sck).ap()
    yscr = nc.dram_tensor("yscr", [NCK, 128, TOK], F32, kind=sck).ap()
    r2scr = nc.dram_tensor("r2scr", [NCK, 128, TOK], F32, kind=sck).ap()
    if debug:
        qdump = nc.dram_tensor("qdump", [128, H * TOK], BF16,
                               kind="ExternalOutput").ap()
        ndump = nc.dram_tensor("ndump", [128, 32], F32,
                               kind="ExternalOutput").ap()
        n2dump = nc.dram_tensor("n2dump", [1, 1024], F32,
                                kind="ExternalOutput").ap()
        ydump = nc.dram_tensor("ydump", [128, NCK * TOK], BF16,
                               kind="ExternalOutput").ap()

    es = ExitStack()
    with es:
        def sb(name, shape, dt):
            return es.enter_context(nc.sbuf_tensor(name, shape, dt, side="left"))

        ones_bf = sb("ones_bf", [128, 1], BF16)
        ones_fc = sb("ones_fc", [128, 1], F32)
        ones_fr = sb("ones_fr", [64, 128], F32)
        bqc = sb("bqc", [128, 48], F32)
        bpc = sb("bpc", [128, 16], F32)
        bqrs = sb("bqrs", [128, 16], F32)
        bkrs = sb("bkrs", [128, 16], F32)
        bvb_es = ExitStack()
        bvb = bvb_es.enter_context(nc.sbuf_tensor("bvb", [128, D], F32,
                                                  side="right"))
        knmax = sb("knmax", [1, 256], F32)
        qnmax = sb("qnmax", [1, 64], F32)
        negc = sb("negc", [128, 16], F32)

        qg = C(es.enter_context(nc.semaphore("qg")))
        qs = C(es.enter_context(nc.semaphore("qs")))
        qa = C(es.enter_context(nc.semaphore("qa")))
        t_a = C(es.enter_context(nc.semaphore("t_a")))
        t_b = C(es.enter_context(nc.semaphore("t_b")))
        t_c = C(es.enter_context(nc.semaphore("t_c")))
        s_a = C(es.enter_context(nc.semaphore("s_a")))
        v_a = C(es.enter_context(nc.semaphore("v_a")))
        v_b = C(es.enter_context(nc.semaphore("v_b")))
        g_a = C(es.enter_context(nc.semaphore("g_a")))
        v_s = C(es.enter_context(nc.semaphore("v_s")))
        v_q = C(es.enter_context(nc.semaphore("v_q")))
        v_q2 = C(es.enter_context(nc.semaphore("v_q2")))
        t_q = C(es.enter_context(nc.semaphore("t_q")))
        sems = [qg, qs, qa, t_a, t_b, t_c, s_a, v_a, v_b, g_a, v_q, v_q2, t_q]

        def vsync(e, instr):
            v_s.inc(instr)
            e.wait_ge(v_s.h, v_s.n)

        def snap():
            return {s: s.n for s in sems}

        def barrier(e, b):
            for s, n in b.items():
                if n:
                    e.wait_ge(s.h, n)

        b = snap()
        with nc.Block() as blk:
            @blk.sync
            def _(e):
                e.dma_start(out=bqc[:], in_=bq_col[:]).then_inc(qs.h, 16)
                e.dma_start(out=bpc[:], in_=bp_col[:]).then_inc(qs.h, 16)
                e.dma_start(out=bvb[0:1, :], in_=bq_row[0:1, 4096:6144]) \
                    .then_inc(qs.h, 16)

            @blk.gpsimd
            def _(e):
                e.memset(ones_bf[:], 1.0)
                e.memset(ones_fc[:], 1.0)
                e.memset(ones_fr[:], 1.0)
                g_a.inc(e.nop())

            @blk.vector
            def _(e):
                e.wait_ge(qs.h, b[qs] + 48)
                e.tensor_scalar(bqrs[:], bqc[:, 0:16], RS, None, ALU.mult)
                v_a.inc(e.tensor_scalar(bkrs[:], bqc[:, 16:32], RS, None,
                                        ALU.mult))
        qs.n += 48

        with nc.Block() as blk, \
                nc.psum_tensor("pstp", [128, 512], F32) as pstp:
            @blk.tensor
            def _(e):
                e.wait_ge(qs.h, b[qs] + 48)
                e.wait_ge(g_a.h, b[g_a] + 1)
                for i in range(4):
                    if i > 0:
                        e.wait_ge(v_a.h, b[v_a] + 1 + i)
                    t_a.inc(e.matmul(pstp[:, 0:512], ones_fr[0:1, :],
                                     bvb[0:1, i * 512:(i + 1) * 512],
                                     start=True, stop=True))

            @blk.vector
            def _(e):
                for i in range(4):
                    e.wait_ge(t_a.h, b[t_a] + i + 1)
                    v_a.inc(e.tensor_copy(bvb[:, i * 512:(i + 1) * 512],
                                          pstp[:, 0:512]))

        QCH = []
        for wi_, (ws_, wd_, nrb_, W_) in enumerate(
                ((w1, w1q, 44, 2048), (w2, w2q, 44, 2048),
                 (w3, w3q, 16, 5632))):
            for rb_ in range(nrb_):
                for co_ in range(0, W_, 2048):
                    QCH.append((ws_, wd_, rb_, co_, min(2048, W_ - co_), wi_))
        NQ = len(QCH)
        racc = sb("racc", [128, NQ], F32)
        gi3 = sb("gi3", [128, 3], F32)
        xm16_es = ExitStack()
        xm16 = xm16_es.enter_context(nc.sbuf_tensor("xm16", [128, NCK * TOK],
                                                    BF16, side="left"))
        r_xm = DmaRing(nc, es, sems, "r_xm", n=1, k=16)
        if upto >= 2 and not _SKIP_MID:
            with ExitStack() as ph:
                def psb(name, shape, dt):
                    return ph.enter_context(nc.sbuf_tensor(name, shape, dt, side="right"))
                wkv = psb("wkv", [128, NCK * 2048], BF16)
                xg = [psb(f"xg{i}", [128, NCK * 512], BF16) for i in range(2)]
                ksb = [psb(f"ksb{i}", [128, 512], BF16) for i in range(2)]
                sqk = [psb(f"sqk{i}", [128, 512], BF16) for i in range(2)]
                vsb = [psb(f"vsb{i}", [128, 512], BF16) for i in range(2)]
                pk = [ph.enter_context(nc.psum_tensor(f"pk{i}", [128, 512], F32))
                      for i in range(2)]
                pv = [ph.enter_context(nc.psum_tensor(f"pv{i}", [128, 512], F32))
                      for i in range(2)]
                pn = [ph.enter_context(nc.psum_tensor(f"pn{i}", [1, 512], F32))
                      for i in range(2)]
                r_wkv = DmaRing(nc, es, sems, "r_wkv", n=1, k=32)
                r_xg = DmaRing(nc, es, sems, "r_xg", n=2, k=16)
                r_kst = DmaRing(nc, es, sems, "r_kst", n=2, k=1)
                r_vst = DmaRing(nc, es, sems, "r_vst", n=2, k=4)
                b = snap()

                with nc.Block() as blk:
                    @blk.gpsimd
                    def _(e):
                        barrier(e, b)
                        for hp in range(2):
                            if hp == 1:
                                e.wait_ge(t_a.h, b[t_a] + 128)
                                e.wait_ge(t_c.h, b[t_c] + 128)
                            for dc in range(NCK):
                                r_wkv.issue(e, hp,
                                            out=wkv[:, dc * 2048:dc * 2048 + 1024],
                                            in_=wqkv[dc * 128:(dc + 1) * 128,
                                                     2048 + hp * 1024:3072 + hp * 1024])
                                r_wkv.issue(e, hp,
                                            out=wkv[:, dc * 2048 + 1024:(dc + 1) * 2048],
                                            in_=wqkv[dc * 128:(dc + 1) * 128,
                                                     4096 + hp * 1024:5120 + hp * 1024])
                            for kg in range(16):
                                s = hp * 16 + kg
                                if s >= 2:
                                    e.wait_ge(t_a.h, b[t_a] + (s - 1) * 8)
                                    e.wait_ge(t_c.h, b[t_c] + (s - 1) * 8)
                                for dc in range(NCK):
                                    r_xg.issue(e, s,
                                               out=xg[s % 2][:, dc * 512:(dc + 1) * 512],
                                               in_=xb[dc, :, kg * 512:(kg + 1) * 512])
                        for dc in range(NCK):
                            r_xm.issue(e, 0, out=xm16[:, dc * TOK:(dc + 1) * TOK],
                                       in_=xm[dc, :, :])
                    @blk.tensor
                    def _(e):
                        barrier(e, b)
                        for hp in range(2):
                            r_wkv.wait(e, hp)
                            for kg in range(16):
                                s = hp * 16 + kg
                                r_xg.wait(e, s)
                                for i in range(8):
                                    idx = s * 8 + i
                                    h_l = i
                                    if idx >= 2:
                                        e.wait_ge(s_a.h, b[s_a] + idx - 1)
                                    for dc in range(NCK):
                                        mm = e.matmul(
                                            pk[idx % 2][:],
                                            wkv[:, dc * 2048 + h_l * 128:dc * 2048 + (h_l + 1) * 128],
                                            xg[s % 2][:, dc * 512:(dc + 1) * 512],
                                            start=(dc == 0), stop=(dc == 15))
                                    t_a.inc(mm)
                                    if i >= 1:
                                        p = idx - 1
                                        e.wait_ge(s_a.h, b[s_a] + p + 1)
                                        if p >= 2:
                                            e.wait_ge(v_a.h, b[v_a] + p - 1)
                                        t_b.inc(e.matmul(pn[p % 2][0:1, :],
                                                         ones_bf[:, 0:1],
                                                         sqk[p % 2][:],
                                                         start=True, stop=True))
                                p = s * 8 + 7
                                e.wait_ge(s_a.h, b[s_a] + p + 1)
                                if p >= 2:
                                    e.wait_ge(v_a.h, b[v_a] + p - 1)
                                t_b.inc(e.matmul(pn[p % 2][0:1, :],
                                                 ones_bf[:, 0:1], sqk[p % 2][:],
                                                 start=True, stop=True))
                                for j in range(8):
                                    jdx = s * 8 + j
                                    g4l, kc = j // 4, j % 4
                                    if jdx >= 2:
                                        e.wait_ge(v_b.h, b[v_b] + jdx - 1)
                                    for dc in range(NCK):
                                        mm = e.matmul(
                                            pv[jdx % 2][:],
                                            xg[s % 2][:, dc * 512 + kc * 128:dc * 512 + (kc + 1) * 128],
                                            wkv[:, dc * 2048 + 1024 + g4l * 512:dc * 2048 + 1024 + (g4l + 1) * 512],
                                            start=(dc == 0), stop=(dc == 15))
                                    t_c.inc(mm)

                    @blk.scalar
                    def _(e):
                        barrier(e, b)
                        for idx in range(256):
                            s = idx // 8
                            i = idx % 8
                            h = (s // 16) * 8 + i
                            e.wait_ge(t_a.h, b[t_a] + idx + 1)
                            if idx >= 2:
                                r_kst.wait(e, idx - 2)
                                e.wait_ge(t_b.h, b[t_b] + idx - 1)
                            e.activation(ksb[idx % 2][:], pk[idx % 2][:],
                                         AF.Identity, bias=bkrs[:, h:h + 1],
                                         scale=RS)
                            s_a.inc(e.activation(sqk[idx % 2][:],
                                                 pk[idx % 2][:], AF.Square,
                                                 bias=bkrs[:, h:h + 1],
                                                 scale=RS))

                    @blk.vector
                    def _(e):
                        barrier(e, b)
                        for hp in range(2):
                            for kg in range(16):
                                s = hp * 16 + kg
                                for i in range(8):
                                    idx = s * 8 + i
                                    h = hp * 8 + i
                                    col = h * 16 + kg
                                    e.wait_ge(t_b.h, b[t_b] + idx + 1)
                                    rm = e.reduce_max(knmax[0:1, col:col + 1],
                                                      pn[idx % 2][0:1, :],
                                                      axis=AX)
                                    vsync(e, rm)
                                    v_a.inc(e.nop())
                                for j in range(8):
                                    jdx = s * 8 + j
                                    g4l = j // 4
                                    e.wait_ge(t_c.h, b[t_c] + jdx + 1)
                                    if jdx >= 2:
                                        r_vst.wait(e, jdx - 2)
                                    v_b.inc(e.tensor_tensor(
                                        vsb[jdx % 2][:], pv[jdx % 2][:],
                                        bvb[:, (hp * 2 + g4l) * 512:(hp * 2 + g4l + 1) * 512],
                                        ALU.add))

                    @blk.sync
                    def _(e):
                        barrier(e, b)
                        for hp in range(2):
                            for kg in range(16):
                                s = hp * 16 + kg
                                for i in range(8):
                                    idx = s * 8 + i
                                    h = hp * 8 + i
                                    e.wait_ge(s_a.h, b[s_a] + idx + 1)
                                    r_kst.issue(
                                        e, idx,
                                        out=kscr[h, :, kg * 512:(kg + 1) * 512],
                                        in_=ksb[idx % 2][:])
                                for j in range(8):
                                    jdx = s * 8 + j
                                    g4l, kc = j // 4, j % 4
                                    g4 = hp * 2 + g4l
                                    e.wait_ge(v_b.h, b[v_b] + jdx + 1)
                                    for hh in range(4):
                                        r_vst.issue(
                                            e, jdx,
                                            out=vscr[g4 * 4 + hh,
                                                     kg * 512 + kc * 128:kg * 512 + (kc + 1) * 128, :],
                                            in_=vsb[jdx % 2][:, hh * 128:(hh + 1) * 128])
                t_a.n = b[t_a] + 256
                t_b.n = b[t_b] + 256
                t_c.n = b[t_c] + 256
                s_a.n = b[s_a] + 256
                v_a.n = b[v_a] + 256
                v_b.n = b[v_b] + 256


        bvb_es.close()

        qt_es = ExitStack()
        qtsb = qt_es.enter_context(nc.sbuf_tensor("qtsb", [128, H * TOK], BF16,
                                                  side="right"))
        if upto >= 3 and not _SKIP_MID:
            with ExitStack() as ph:
                def psb(name, shape, dt):
                    return ph.enter_context(nc.sbuf_tensor(name, shape, dt, side="right"))
                wqh = [psb(f"wqh{i}", [128, NCK * 128], BF16) for i in range(2)]
                sqq = [psb(f"sqq{i}", [128, 512], BF16) for i in range(2)]
                pq = [ph.enter_context(nc.psum_tensor(f"pq{i}", [128, 512], F32))
                      for i in range(2)]
                pnq = [ph.enter_context(nc.psum_tensor(f"pnq{i}", [1, 512], F32))
                       for i in range(2)]
                r_wq = DmaRing(nc, es, sems, "r_wq", n=2, k=1)
                b = snap()

                with nc.Block() as blk:
                    @blk.gpsimd
                    def _(e):
                        barrier(e, b)
                        for h in range(H):
                            if h >= 2:
                                e.wait_ge(t_a.h, b[t_a] + (h - 1) * 4)
                            r_wq.issue(
                                e, h,
                                out=wqh[h % 2][:].rearrange("p (a j) -> p a j", a=NCK),
                                in_=wqkv[:, h * 128:(h + 1) * 128]
                                .rearrange("(a p) j -> p a j", p=128))

                    @blk.tensor
                    def _(e):
                        barrier(e, b)
                        r_xm.wait(e, 0)
                        for h in range(H):
                            r_wq.wait(e, h)
                            for qt in range(4):
                                n = h * 4 + qt
                                if n >= 2:
                                    e.wait_ge(s_a.h, b[s_a] + n - 1)
                                for dc in range(NCK):
                                    mm = e.matmul(
                                        pq[n % 2][:],
                                        wqh[h % 2][:, dc * 128:(dc + 1) * 128],
                                        xm16[:, dc * TOK + qt * 512:dc * TOK + (qt + 1) * 512],
                                        start=(dc == 0), stop=(dc == 15))
                                t_a.inc(mm)
                                if n >= 1:
                                    p = n - 1
                                    e.wait_ge(s_a.h, b[s_a] + p + 1)
                                    if p >= 2:
                                        e.wait_ge(v_a.h, b[v_a] + p - 1)
                                    t_b.inc(e.matmul(pnq[p % 2][0:1, :],
                                                     ones_bf[:, 0:1],
                                                     sqq[p % 2][:],
                                                     start=True, stop=True))
                        e.wait_ge(s_a.h, b[s_a] + 64)
                        e.wait_ge(v_a.h, b[v_a] + 62)
                        t_b.inc(e.matmul(pnq[63 % 2][0:1, :], ones_bf[:, 0:1],
                                         sqq[63 % 2][:], start=True, stop=True))

                    @blk.scalar
                    def _(e):
                        barrier(e, b)
                        for n in range(64):
                            h, qt = n // 4, n % 4
                            e.wait_ge(t_a.h, b[t_a] + n + 1)
                            if n >= 2:
                                e.wait_ge(t_b.h, b[t_b] + n - 1)
                            e.activation(
                                qtsb[:, h * TOK + qt * 512:h * TOK + (qt + 1) * 512],
                                pq[n % 2][:], AF.Identity,
                                bias=bqrs[:, h:h + 1], scale=RS)
                            s_a.inc(e.activation(
                                sqq[n % 2][:], pq[n % 2][:], AF.Square,
                                bias=bqrs[:, h:h + 1], scale=RS))

                    @blk.vector
                    def _(e):
                        barrier(e, b)
                        for n in range(64):
                            e.wait_ge(t_b.h, b[t_b] + n + 1)
                            rm = e.reduce_max(qnmax[0:1, n:n + 1],
                                              pnq[n % 2][0:1, :], axis=AX)
                            vsync(e, rm)
                            v_a.inc(e.nop())
                t_a.n = b[t_a] + 64
                t_b.n = b[t_b] + 64
                s_a.n = b[s_a] + 64
                v_a.n = b[v_a] + 64

            with ExitStack() as ph, \
                    nc.psum_tensor("pnc", [128, 16], F32) as pnc:
                km2 = ph.enter_context(nc.sbuf_tensor("km2", [1, 16], F32, side="right"))
                qm2 = ph.enter_context(nc.sbuf_tensor("qm2", [1, 16], F32, side="right"))
                ncr = ph.enter_context(nc.sbuf_tensor("ncr", [1, 16], F32, side="right"))
                ncs = ph.enter_context(nc.sbuf_tensor("ncs", [1, 16], F32, side="right"))
                b = snap()
                with nc.Block() as blk:
                    @blk.vector
                    def _(e):
                        barrier(e, b)
                        for h in range(H):
                            vsync(e, e.reduce_max(km2[0:1, h:h + 1],
                                                  knmax[0:1, h * 16:(h + 1) * 16],
                                                  axis=AX))
                            vsync(e, e.reduce_max(qm2[0:1, h:h + 1],
                                                  qnmax[0:1, h * 4:(h + 1) * 4],
                                                  axis=AX))
                        vsync(e, e.tensor_tensor(ncr[0:1, :], km2[0:1, :],
                                                 qm2[0:1, :], ALU.add))
                        v_a.inc(e.tensor_scalar(ncs[0:1, :], ncr[0:1, :], -0.5,
                                                None, ALU.mult))
                        e.wait_ge(t_a.h, b[t_a] + 1)
                        v_a.inc(e.tensor_copy(negc[:], pnc[:, 0:16]))

                    @blk.tensor
                    def _(e):
                        e.wait_ge(v_a.h, b[v_a] + 1)
                        t_a.inc(e.matmul(pnc[:, 0:16], ones_fr[0:1, :],
                                         ncs[0:1, 0:16], start=True, stop=True))

                    if debug:
                        @blk.sync
                        def _(e):
                            e.wait_ge(v_a.h, b[v_a] + 2)
                            e.dma_start(out=n2dump[0:1, 320:336], in_=km2[0:1, :]) \
                                .then_inc(qs.h, 16)
                            e.dma_start(out=n2dump[0:1, 336:352], in_=qm2[0:1, :]) \
                                .then_inc(qs.h, 16)
                            e.dma_start(out=n2dump[0:1, 352:368], in_=ncs[0:1, :]) \
                                .then_inc(qs.h, 16)
                            e.dma_start(out=n2dump[0:1, 368:384], in_=ncr[0:1, :]) \
                                .then_inc(qs.h, 16)
                        qs.n += 64

            if debug and upto == 3:
                b = snap()
                with nc.Block() as blk:
                    @blk.sync
                    def _(e):
                        barrier(e, b)
                        e.dma_start(out=qdump[:], in_=qtsb[:]).then_inc(qs.h, 16)
                        e.dma_start(out=ndump[:, 0:16], in_=negc[:]) \
                            .then_inc(qs.h, 16)
                        e.dma_start(out=n2dump[0:1, 0:256], in_=knmax[0:1, :]) \
                            .then_inc(qs.h, 16)
                        e.dma_start(out=n2dump[0:1, 256:320], in_=qnmax[0:1, :]) \
                            .then_inc(qs.h, 16)
                        e.wait_ge(qs.h, b[qs] + 64)
                qs.n += 64


        xm16_es.close()

        if upto >= 4 and not _SKIP_MID:
            with ExitStack() as ph:
                def psb(name, shape, dt):
                    return ph.enter_context(nc.sbuf_tensor(name, shape, dt, side="right"))
                kh = [psb(f"kh{i}", [128, LK], BF16) for i in range(2)]
                vh = [psb(f"vh{i}", [128, LK], BF16) for i in range(2)]
                exs = [psb(f"exs{i}", [128, 1024], BF16) for i in range(4)]
                rlsb = psb("rlsb", [64, 512], F32)
                rbsb = psb("rbsb", [128, 1024], F32)
                ps_s = [ph.enter_context(nc.psum_tensor(f"ps_s{i}", [128, 1024], F32))
                        for i in range(2)]
                po = [ph.enter_context(nc.psum_tensor(f"po{i}", [128, 512], F32))
                      for i in range(2)]
                pl = [ph.enter_context(nc.psum_tensor(f"pl{i}", [128, 512], F32))
                      for i in range(2)]
                wbuf2 = [psb(f"wbuf2_{i}", [128, 2048], F32) for i in range(3)]
                tq1 = psb("tq1", [128, 2048], BF16)
                tq2 = psb("tq2", [128, 2048], BF16)
                wqb2 = [psb(f"wqb2_{i}", [128, 2048], BF16) for i in range(2)]
                rtot3 = psb("rtot3", [128, 3], F32)
                gsc3 = psb("gsc3", [1, 3], F32)
                grc3 = psb("grc3", [1, 3], F32)
                r_kv = DmaRing(nc, es, sems, "r_kv", n=2, k=2)
                r_q1 = DmaRing(nc, es, sems, "r_q1", n=3, k=1)
                r_q2 = DmaRing(nc, es, sems, "r_q2", n=3, k=1)
                r_q2s = DmaRing(nc, es, sems, "r_q2s", n=2, k=1)
                NW = 44 * 128 * 2048
                _q1c = [23, 23, 23, 23, 22, 22] + [0] * 10
                q1_of_h = []
                _qa = 0
                for h_ in range(H):
                    q1_of_h.append(list(range(_qa, _qa + _q1c[h_])))
                    _qa += _q1c[h_]
                q2_of_h = [list(range((h_ - 7) * 17, (h_ - 6) * 17))
                           if 7 <= h_ <= 14 else [] for h_ in range(H)]
                b = snap()

                with nc.Block() as blk:
                    @blk.sync
                    def _(e):
                        barrier(e, b)
                        for h in range(H):
                            if h >= 2:
                                e.wait_ge(t_a.h, b[t_a] + (h - 1) * 128)
                                e.wait_ge(t_b.h, b[t_b] + (h - 1) * 128)
                            r_kv.issue(e, h, out=kh[h % 2][:], in_=kscr[h, :, :])
                            r_kv.issue(
                                e, h,
                                out=vh[h % 2][:].rearrange("p (a d) -> p a d", a=64),
                                in_=vscr[h].rearrange("(a p) d -> p a d", p=128))

                    @blk.gpsimd
                    def _(e):
                        barrier(e, b)
                        for ci in range(NQ):
                            ws_, wd_, rb_, co_, wdt_, wi_ = QCH[ci]
                            if ci >= 3:
                                e.wait_ge(v_q.h, b[v_q] + ci - 2)
                            r_q1.issue(e, ci, out=wbuf2[ci % 3][:, 0:wdt_],
                                       in_=ws_[rb_ * 128:(rb_ + 1) * 128,
                                               co_:co_ + wdt_])
                        e.wait_ge(v_q.h, b[v_q] + NQ)
                        for ci in range(NQ):
                            ws_, wd_, rb_, co_, wdt_, wi_ = QCH[ci]
                            if ci >= 3:
                                e.wait_ge(v_q2.h, b[v_q2] + ci - 2)
                            r_q2.issue(e, ci, out=wbuf2[ci % 3][:, 0:wdt_],
                                       in_=ws_[rb_ * 128:(rb_ + 1) * 128,
                                               co_:co_ + wdt_])
                            if ci >= 2:
                                up = ci - 2
                                us_, ud_, urb, uco, uwd, uwi = QCH[up]
                                e.wait_ge(v_q2.h, b[v_q2] + up + 1)
                                r_q2s.issue(e, up,
                                            out=ud_[urb * 128:(urb + 1) * 128,
                                                    uco:uco + uwd],
                                            in_=wqb2[up % 2][:, 0:uwd])
                        e.wait_ge(v_q2.h, b[v_q2] + NQ)
                        for up in (NQ - 2, NQ - 1):
                            us_, ud_, urb, uco, uwd, uwi = QCH[up]
                            r_q2s.issue(e, up,
                                        out=ud_[urb * 128:(urb + 1) * 128,
                                                uco:uco + uwd],
                                        in_=wqb2[up % 2][:, 0:uwd])

                    @blk.tensor
                    def _(e):
                        barrier(e, b)
                        for h in range(H):
                            if h == 7:
                                e.wait_ge(v_q.h, b[v_q] + NQ + 3)
                            for qp in range(2):
                                g = h * 2 + qp
                                eb = b[s_a] + g * 64
                                if qp == 0:
                                    r_kv.wait(e, h)
                                for kc in range(64):
                                    for q2 in range(2):
                                        mm = e.matmul(
                                            ps_s[kc % 2][:, q2 * 512:(q2 + 1) * 512],
                                            kh[h % 2][:, kc * 128:(kc + 1) * 128],
                                            qtsb[:, h * TOK + qp * 1024 + q2 * 512:
                                                 h * TOK + qp * 1024 + (q2 + 1) * 512],
                                            start=True, stop=True)
                                    t_a.inc(mm)
                                    if kc == 1 and g >= 1:
                                        if g != 14:
                                            e.wait_ge(v_a.h, b[v_a] + g)
                                            e.matmul(
                                                pl[1][:], ones_fr[0:1, :],
                                                rlsb[0:1, :], start=True,
                                                stop=True,
                                                tile_position=(0, 0))
                                            mm = e.matmul(
                                                pl[0][:], ones_fr[32:33, :],
                                                rlsb[32:33, :], start=True,
                                                stop=True,
                                                tile_position=(32, 0))
                                            t_c.inc(mm)
                                        e.wait_ge(v_b.h, b[v_b] + g)
                                    if kc >= 1:
                                        p = kc - 1
                                        e.wait_ge(s_a.h, eb + p + 1)
                                        for q2 in range(2):
                                            e.matmul(po[q2][:],
                                                     vh[h % 2][:, p * 128:(p + 1) * 128],
                                                     exs[p % 4][:, q2 * 512:(q2 + 1) * 512],
                                                     start=(p == 0), stop=False)
                                        for q2 in range(2):
                                            mm = e.matmul(pl[0][q2 * 32:q2 * 32 + 1, :],
                                                          ones_bf[:, 0:1],
                                                          exs[p % 4][:, q2 * 512:(q2 + 1) * 512],
                                                          start=(p == 0), stop=False,
                                                          tile_position=(0, q2 * 32))
                                        t_b.inc(mm)
                                e.wait_ge(s_a.h, eb + 64)
                                for q2 in range(2):
                                    e.matmul(po[q2][:], vh[h % 2][:, 63 * 128:64 * 128],
                                             exs[63 % 4][:, q2 * 512:(q2 + 1) * 512],
                                             start=False, stop=True)
                                for q2 in range(2):
                                    mm = e.matmul(pl[0][q2 * 32:q2 * 32 + 1, :],
                                                  ones_bf[:, 0:1],
                                                  exs[63 % 4][:, q2 * 512:(q2 + 1) * 512],
                                                  start=False, stop=True,
                                                  tile_position=(0, q2 * 32))
                                t_b.inc(mm)
                            if h == 6:
                                e.wait_ge(v_a.h, b[v_a] + 14)
                                e.matmul(pl[1][:], ones_fr[0:1, :],
                                         rlsb[0:1, :], start=True, stop=True,
                                         tile_position=(0, 0))
                                mm = e.matmul(pl[0][:], ones_fr[32:33, :],
                                              rlsb[32:33, :], start=True,
                                              stop=True,
                                              tile_position=(32, 0))
                                t_c.inc(mm)
                                e.wait_ge(v_q.h, b[v_q] + NQ + 1)
                                t_q.inc(e.matmul(ps_s[0][0:1, 0:3],
                                                 ones_fc[:, 0:1], rtot3[:, 0:3],
                                                 start=True, stop=True))
                                e.wait_ge(v_q.h, b[v_q] + NQ + 2)
                                t_q.inc(e.matmul(ps_s[1][:, 0:3], ones_fr[0:1, :],
                                                 grc3[0:1, 0:3],
                                                 start=True, stop=True))
                        e.wait_ge(v_a.h, b[v_a] + 32)
                        e.matmul(pl[1][:], ones_fr[0:1, :], rlsb[0:1, :],
                                 start=True, stop=True, tile_position=(0, 0))
                        mm = e.matmul(pl[0][:], ones_fr[32:33, :],
                                      rlsb[32:33, :], start=True, stop=True,
                                      tile_position=(32, 0))
                        t_c.inc(mm)

                    @blk.scalar
                    def _(e):
                        barrier(e, b)
                        for h in range(H):
                            for qp in range(2):
                                g = h * 2 + qp
                                for kc in range(64):
                                    e.wait_ge(t_a.h, b[t_a] + g * 64 + kc + 1)
                                    if kc >= 4:
                                        e.wait_ge(t_b.h, b[t_b] + g * 64 + kc - 3)
                                    s_a.inc(e.activation(exs[kc % 4][:],
                                                         ps_s[kc % 2][:], AF.Exp,
                                                         bias=negc[:, h:h + 1]))

                    @blk.vector
                    def _(e):
                        barrier(e, b)
                        for h in range(H):
                            for qp in range(2):
                                g = h * 2 + qp
                                e.wait_ge(t_b.h, b[t_b] + (g + 1) * 64)
                                e.reciprocal(rlsb[0:1, :], pl[0][0:1, :])
                                v_a.inc(e.reciprocal(rlsb[32:33, :],
                                                     pl[0][32:33, :]))
                                e.wait_ge(t_c.h, b[t_c] + g + 1)
                                e.tensor_copy(rbsb[:, 0:512], pl[1][:])
                                vsync(e, e.tensor_copy(rbsb[:, 512:1024],
                                                       pl[0][:]))
                                for q2 in range(2):
                                    col = h * TOK + qp * 1024 + q2 * 512
                                    mm = e.tensor_tensor(
                                        qtsb[:, col:col + 512], po[q2][:],
                                        rbsb[:, q2 * 512:(q2 + 1) * 512],
                                        ALU.mult)
                                v_b.inc(mm)
                                e.wait_ge(v_b.h, v_b.n)
                            if qp == 1:
                                for ci in q1_of_h[h]:
                                    wdt_ = QCH[ci][4]
                                    r_q1.wait(e, ci)
                                    rs = e.reduce_sum(racc[:, ci:ci + 1],
                                                      wbuf2[ci % 3][:, 0:wdt_],
                                                      axis=AX,
                                                      apply_absolute_value=True)
                                    vsync(e, rs)
                                    v_q.inc(e.nop())
                                if h == 6:
                                    vsync(e, e.reduce_sum(rtot3[:, 0:1],
                                                          racc[:, 0:44], axis=AX))
                                    vsync(e, e.reduce_sum(rtot3[:, 1:2],
                                                          racc[:, 44:88], axis=AX))
                                    vsync(e, e.reduce_sum(rtot3[:, 2:3],
                                                          racc[:, 88:NQ], axis=AX))
                                    v_q.inc(e.nop())
                                    e.wait_ge(t_q.h, b[t_q] + 1)
                                    vsync(e, e.tensor_scalar(
                                        gsc3[0:1, 0:3], ps_s[0][0:1, 0:3],
                                        1.0 / NW, EPS_Q, ALU.mult, ALU.add))
                                    vsync(e, e.reciprocal(grc3[0:1, 0:3],
                                                          gsc3[0:1, 0:3]))
                                    v_q.inc(e.nop())
                                    e.wait_ge(t_q.h, b[t_q] + 2)
                                    vsync(e, e.tensor_copy(gi3[:],
                                                           ps_s[1][:, 0:3]))
                                    v_q.inc(e.nop())
                                for ci in q2_of_h[h]:
                                    ws_, wd_, rb_, co_, wdt_, wi_ = QCH[ci]
                                    r_q2.wait(e, ci)
                                    if ci >= 2:
                                        r_q2s.wait(e, ci - 2)
                                    vsync(e, e.tensor_scalar(
                                        tq1[:, 0:wdt_], wbuf2[ci % 3][:, 0:wdt_],
                                        gi3[:, wi_:wi_ + 1], 0.5, ALU.mult,
                                        ALU.is_gt))
                                    vsync(e, e.tensor_scalar(
                                        tq2[:, 0:wdt_], wbuf2[ci % 3][:, 0:wdt_],
                                        gi3[:, wi_:wi_ + 1], -0.5, ALU.mult,
                                        ALU.is_lt))
                                    v_q2.inc(e.tensor_tensor(
                                        wqb2[ci % 2][:, 0:wdt_], tq1[:, 0:wdt_],
                                        tq2[:, 0:wdt_], ALU.subtract))
                                    e.wait_ge(v_q2.h, v_q2.n)
                t_a.n = b[t_a] + 2048
                t_b.n = b[t_b] + 2048
                t_c.n = b[t_c] + 32
                s_a.n = b[s_a] + 2048
                v_a.n = b[v_a] + 32
                v_b.n = b[v_b] + 32
                v_q.n = b[v_q] + NQ + 3
                v_q2.n = b[v_q2] + NQ
                t_q.n = b[t_q] + 2

            if debug:
                b = snap()
                with nc.Block() as blk:
                    @blk.sync
                    def _(e):
                        barrier(e, b)
                        e.dma_start(out=qdump[:], in_=qtsb[:]).then_inc(qs.h, 16)
                        e.dma_start(out=ndump[:, 0:16], in_=negc[:]) \
                            .then_inc(qs.h, 16)
                        e.wait_ge(qs.h, b[qs] + 32)
                qs.n += 32

        if upto >= 5 and not _SKIP_MID:
            with ExitStack() as ph:
                def psb(name, shape, dt):
                    return ph.enter_context(nc.sbuf_tensor(name, shape, dt, side="right"))
                wp = [psb(f"wp{i}", [128, NCK * 128], BF16) for i in range(3)]
                xm32 = [psb(f"xm32_{i}", [128, 512], F32) for i in range(3)]
                prs = [psb(f"prs{i}", [128, 512], F32) for i in range(3)]
                res1q = [psb(f"res1q{i}", [128, NCK * 512], F32)
                         for i in range(2)]
                sqt = [psb(f"sqt{i}", [128, 512], F32) for i in range(3)]
                m_all = psb("m_all", [1, 512], F32)
                r_all = psb("r_all", [1, 512], F32)
                tmsq = psb("tmsq", [1, 512], F32)
                tm2 = psb("tm2", [1, 512], F32)
                tm3 = psb("tm3", [1, 512], F32)
                tm4 = tm2
                rtmp = tm3
                ttmp = [psb("ttmp0", [128, 512], F32)] * 2
                y32 = [psb(f"y32_{i}", [128, 512], F32) for i in range(2)]
                pp = [ph.enter_context(nc.psum_tensor(f"pp{i}", [128, 512], F32))
                      for i in range(3)]
                pst_s = ph.enter_context(nc.psum_tensor("pst_s", [1, 512], F32))
                pst_q = ph.enter_context(nc.psum_tensor("pst_q", [1, 512], F32))
                pmb = ph.enter_context(nc.psum_tensor("pmb", [128, 512], F32))
                prb = ph.enter_context(nc.psum_tensor("prb", [128, 512], F32))
                r_wp = DmaRing(nc, es, sems, "r_wp", n=3, k=1)
                r_x32 = DmaRing(nc, es, sems, "r_x32", n=3, k=1)
                r_y32 = DmaRing(nc, es, sems, "r_y32", n=2, k=1)
                b = snap()

                def SA_evac(t):
                    return b[s_a] + (t // 16) * 17 + (t % 16) + 1

                def VA_add(t):
                    return b[v_a] + (t // 16) * 17 + (t % 16) + 1

                with nc.Block() as blk:
                    @blk.gpsimd
                    def _(e):
                        barrier(e, b)
                        for qt in range(4):
                            for dco in range(NCK):
                                t = qt * 16 + dco
                                if t >= 3:
                                    e.wait_ge(t_a.h, b[t_a] + t - 2)
                                r_wp.issue(
                                    e, t,
                                    out=wp[t % 3][:].rearrange("p (a o) -> p a o", a=NCK),
                                    in_=wproj[:, dco * 128:(dco + 1) * 128]
                                    .rearrange("(a p) o -> p a o", p=128))

                    @blk.sync
                    def _(e):
                        barrier(e, b)
                        def ystores(nq):
                            for dco in range(NCK):
                                t = nq * 16 + dco
                                e.wait_ge(v_b.h, b[v_b] + t + 1)
                                r_y32.issue(e, t,
                                            out=yscr[dco, :, nq * 512:(nq + 1) * 512],
                                            in_=y32[dco % 2][:])
                        for qt in range(4):
                            for dco in range(NCK):
                                t = qt * 16 + dco
                                if t >= 3:
                                    e.wait_ge(v_a.h, VA_add(t - 3))
                                r_x32.issue(e, t, out=xm32[t % 3][:],
                                            in_=xm[dco, :, qt * 512:(qt + 1) * 512])
                            if qt >= 1:
                                ystores(qt - 1)
                        ystores(3)

                    @blk.tensor
                    def _(e):
                        barrier(e, b)
                        def bcast(pq):
                            e.wait_ge(s_a.h, b[s_a] + (pq + 1) * 17)
                            if pq >= 1:
                                e.wait_ge(v_b.h, b[v_b] + pq * 16)
                            e.matmul(pmb[:], ones_fr[0:1, :], m_all[0:1, :],
                                     start=True, stop=True)
                            mm = e.matmul(prb[:], ones_fr[0:1, :], r_all[0:1, :],
                                          start=True, stop=True)
                            t_c.inc(mm)
                        for qt in range(4):
                            if qt >= 1:
                                bcast(qt - 1)
                            for dco in range(NCK):
                                t = qt * 16 + dco
                                r_wp.wait(e, t)
                                if t >= 3:
                                    e.wait_ge(s_a.h, SA_evac(t - 3))
                                for dc in range(NCK):
                                    mm = e.matmul(
                                        pp[t % 3][:],
                                        wp[t % 3][:, dc * 128:(dc + 1) * 128],
                                        qtsb[:, dc * TOK + qt * 512:dc * TOK + (qt + 1) * 512],
                                        start=(dc == 0), stop=(dc == 15))
                                t_a.inc(mm)
                                if dco >= 2:
                                    pdc = dco - 2
                                    pt = qt * 16 + pdc
                                    e.wait_ge(v_a.h, VA_add(pt))
                                    e.matmul(pst_s[0:1, :], ones_fc[:, 0:1],
                                             res1q[qt % 2][:, pdc * 512:(pdc + 1) * 512],
                                             start=(pdc == 0), stop=False)
                                    mm = e.matmul(pst_q[0:1, :], ones_fc[:, 0:1],
                                                  sqt[pt % 3][:],
                                                  start=(pdc == 0), stop=False)
                                    t_b.inc(mm)
                            for pdc in (14, 15):
                                pt = qt * 16 + pdc
                                e.wait_ge(v_a.h, VA_add(pt))
                                e.matmul(pst_s[0:1, :], ones_fc[:, 0:1],
                                         res1q[qt % 2][:, pdc * 512:(pdc + 1) * 512],
                                         start=False, stop=(pdc == 15))
                                mm = e.matmul(pst_q[0:1, :], ones_fc[:, 0:1],
                                              sqt[pt % 3][:],
                                              start=False, stop=(pdc == 15))
                                t_b.inc(mm)
                        bcast(3)

                    @blk.scalar
                    def _(e):
                        barrier(e, b)
                        for qt in range(4):
                            for dco in range(NCK):
                                t = qt * 16 + dco
                                e.wait_ge(t_a.h, b[t_a] + t + 1)
                                if t >= 3:
                                    e.wait_ge(v_a.h, VA_add(t - 3))
                                s_a.inc(e.activation(prs[t % 3][:], pp[t % 3][:],
                                                     AF.Identity,
                                                     bias=bpc[:, dco:dco + 1]))
                            e.wait_ge(v_a.h, b[v_a] + qt * 17 + 17)
                            s_a.inc(e.activation(
                                r_all[0:1, :],
                                rtmp[0:1, :], AF.Sqrt))

                    @blk.vector
                    def _(e):
                        barrier(e, b)
                        def normalize(nq):
                            for dco in range(NCK):
                                t = nq * 16 + dco
                                if t >= 2:
                                    r_y32.wait(e, t - 2)
                                vsync(e, e.tensor_tensor(
                                    ttmp[dco % 2][:],
                                    res1q[nq % 2][:, dco * 512:(dco + 1) * 512],
                                    pmb[:], ALU.subtract))
                                v_b.inc(e.tensor_tensor(y32[dco % 2][:],
                                                        ttmp[dco % 2][:],
                                                        prb[:], ALU.mult))
                                e.wait_ge(v_b.h, v_b.n)
                        for qt in range(4):
                            for dco in range(NCK):
                                t = qt * 16 + dco
                                e.wait_ge(s_a.h, SA_evac(t))
                                r_x32.wait(e, t)
                                vsync(e, e.tensor_tensor(
                                    res1q[qt % 2][:, dco * 512:(dco + 1) * 512],
                                    prs[t % 3][:], xm32[t % 3][:], ALU.add))
                                v_a.inc(e.tensor_tensor(
                                    sqt[t % 3][:],
                                    res1q[qt % 2][:, dco * 512:(dco + 1) * 512],
                                    res1q[qt % 2][:, dco * 512:(dco + 1) * 512],
                                    ALU.mult))
                            e.wait_ge(t_b.h, b[t_b] + (qt + 1) * 16)
                            vsync(e, e.tensor_scalar(m_all[0:1, :],
                                                     pst_s[0:1, :], 1.0 / D,
                                                     None, ALU.mult))
                            vsync(e, e.tensor_scalar(tmsq[0:1, :], pst_q[0:1, :],
                                                     1.0 / D, None, ALU.mult))
                            vsync(e, e.tensor_tensor(
                                tm2[0:1, :],
                                m_all[0:1, :],
                                m_all[0:1, :], ALU.mult))
                            vsync(e, e.tensor_tensor(tm3[0:1, :], tmsq[0:1, :],
                                                     tm2[0:1, :], ALU.subtract))
                            vsync(e, e.tensor_scalar(tm4[0:1, :], tm3[0:1, :],
                                                     EPS_LN, None, ALU.add))
                            v_a.inc(e.reciprocal(rtmp[0:1, :], tm4[0:1, :]))
                            if qt >= 1:
                                e.wait_ge(t_c.h, b[t_c] + qt)
                                normalize(qt - 1)
                        e.wait_ge(t_c.h, b[t_c] + 4)
                        normalize(3)
                t_a.n = b[t_a] + 64
                t_b.n = b[t_b] + 64
                t_c.n = b[t_c] + 4
                s_a.n = b[s_a] + 68
                v_a.n = b[v_a] + 68
                v_b.n = b[v_b] + 64

            if debug:
                b = snap()
                with nc.Block() as blk:
                    @blk.sync
                    def _(e):
                        barrier(e, b)
                        e.dma_start(out=ydump[:], in_=ysb[:]).then_inc(qs.h, 16)
                        e.wait_ge(qs.h, b[qs] + 16)
                qs.n += 16
        qt_es.close()

        if upto >= 7:
            hidden = sb("hidden", [128, JCK * 1024], BF16)
            ysbh = sb("ysbh", [128, NCK * 1024], BF16)
            r_ysb = DmaRing(nc, es, sems, "r_ysb", n=1, k=16)
            r_w12 = DmaRing(nc, es, sems, "r_w12", n=2, k=32)
            r_w3 = DmaRing(nc, es, sems, "r_w3", n=4, k=1)
            r_yl = DmaRing(nc, es, sems, "r_yl", n=2, k=1)
            r_r2s = DmaRing(nc, es, sems, "r_r2s", n=2, k=1)
            r_rl = DmaRing(nc, es, sems, "r_rl", n=2, k=1)
            r_out = DmaRing(nc, es, sems, "r_out", n=2, k=1)
            for th in range(_FF_TH):
                toff = th * 1024
                th_es = ExitStack()
                m2s = th_es.enter_context(
                    nc.sbuf_tensor(f"m2s{th}", [1, 1024], F32, side="right"))
                r2s = th_es.enter_context(
                    nc.sbuf_tensor(f"r2s{th}", [1, 1024], F32, side="right"))
                with ExitStack() as ph:
                    def psb(name, shape, dt):
                        return ph.enter_context(nc.sbuf_tensor(name, shape, dt, side="right"))
                    w1t = [psb(f"w1t{th}_{i}", [128, NCK * 512], BF16)
                           for i in range(2)]
                    w2t = [psb(f"w2t{th}_{i}", [128, NCK * 512], BF16)
                           for i in range(2)]
                    s1s = [psb(f"s1s{th}_{i}", [128, 1024], BF16)
                           for i in range(2)]
                    px1 = [ph.enter_context(
                        nc.psum_tensor(f"px1{th}_{i}", [128, 1024], F32))
                        for i in range(2)]
                    px2 = [ph.enter_context(
                        nc.psum_tensor(f"px2{th}_{i}", [128, 1024], F32))
                        for i in range(2)]
                    b = snap()
                    with nc.Block() as blk:
                        @blk.gpsimd
                        def _(e):
                            barrier(e, b)
                            for dc in range(NCK):
                                r_ysb.issue(e, th,
                                            out=ysbh[:, dc * 1024:(dc + 1) * 1024],
                                            in_=yscr[dc, :, toff:toff + 1024])

                        @blk.sync
                        def _(e):
                            barrier(e, b)
                            for jg in range(11):
                                if jg >= 2:
                                    e.wait_ge(t_a.h, b[t_a] + (jg - 1) * 4)
                                for dc in range(NCK):
                                    r_w12.issue(
                                        e, th * 11 + jg,
                                        out=w1t[jg % 2][:, dc * 512:(dc + 1) * 512],
                                        in_=w1q[jg * 512:(jg + 1) * 512,
                                                dc * 128:(dc + 1) * 128],
                                        transpose=True)
                                for dc in range(NCK):
                                    r_w12.issue(
                                        e, th * 11 + jg,
                                        out=w2t[jg % 2][:, dc * 512:(dc + 1) * 512],
                                        in_=w2q[jg * 512:(jg + 1) * 512,
                                                dc * 128:(dc + 1) * 128],
                                        transpose=True)

                        @blk.tensor
                        def _(e):
                            barrier(e, b)
                            r_ysb.wait(e, th)
                            for jg in range(11):
                                r_w12.wait(e, th * 11 + jg)
                                for jc2 in range(4):
                                    jc = jg * 4 + jc2
                                    if jc >= 2:
                                        e.wait_ge(v_a.h, b[v_a] + jc - 1)
                                    for q2 in range(2):
                                        for dc in range(NCK):
                                            e.matmul(
                                                px1[jc % 2][:, q2 * 512:(q2 + 1) * 512],
                                                w1t[jg % 2][:, dc * 512 + jc2 * 128:
                                                             dc * 512 + jc2 * 128 + 128],
                                                ysbh[:, dc * 1024 + q2 * 512:
                                                     dc * 1024 + (q2 + 1) * 512],
                                                start=(dc == 0), stop=(dc == 15))
                                    for q2 in range(2):
                                        for dc in range(NCK):
                                            mm = e.matmul(
                                                px2[jc % 2][:, q2 * 512:(q2 + 1) * 512],
                                                w2t[jg % 2][:, dc * 512 + jc2 * 128:
                                                             dc * 512 + jc2 * 128 + 128],
                                                ysbh[:, dc * 1024 + q2 * 512:
                                                     dc * 1024 + (q2 + 1) * 512],
                                                start=(dc == 0), stop=(dc == 15))
                                    t_a.inc(mm)

                        @blk.scalar
                        def _(e):
                            barrier(e, b)
                            for jc in range(JCK):
                                e.wait_ge(t_a.h, b[t_a] + jc + 1)
                                if jc >= 2:
                                    e.wait_ge(v_a.h, b[v_a] + jc - 1)
                                s_a.inc(e.activation(s1s[jc % 2][:],
                                                     px1[jc % 2][:],
                                                     AF.Sigmoid))

                        @blk.vector
                        def _(e):
                            barrier(e, b)
                            for jc in range(JCK):
                                e.wait_ge(s_a.h, b[s_a] + jc + 1)
                                v_a.inc(e.tensor_tensor(
                                    hidden[:, jc * 1024:(jc + 1) * 1024],
                                    s1s[jc % 2][:], px2[jc % 2][:], ALU.mult))
                    t_a.n = b[t_a] + JCK
                    s_a.n = b[s_a] + JCK
                    v_a.n = b[v_a] + JCK

                if _FF_PARTS < 2:
                    th_es.close()
                    continue
                with ExitStack() as ph:
                    def psb(name, shape, dt):
                        return ph.enter_context(nc.sbuf_tensor(name, shape, dt, side="right"))
                    w3t = [psb(f"w3t{th}_{i}", [128, 256], BF16) for i in range(4)]
                    y32l = [psb(f"y32l{th}_{i}", [128, 512], F32) for i in range(2)]
                    r2sb = [psb(f"r2sb{th}_{i}", [128, 512], F32) for i in range(2)]
                    r2sq = [psb(f"r2sq{th}_{i}", [128, 512], F32) for i in range(2)]
                    tms = psb(f"tms{th}", [1, 512], F32)
                    tmm = psb(f"tmm{th}", [1, 512], F32)
                    tmn = psb(f"tmn{th}", [1, 512], F32)
                    tmo = psb(f"tmo{th}", [1, 512], F32)
                    rtm2 = psb(f"rtm2{th}", [1, 1024], F32)
                    px3 = [ph.enter_context(
                        nc.psum_tensor(f"px3{th}_{i}", [128, 1024], F32))
                        for i in range(2)]
                    pls = [ph.enter_context(
                        nc.psum_tensor(f"pls{th}_{i}", [1, 512], F32))
                        for i in range(2)]
                    plq = [ph.enter_context(
                        nc.psum_tensor(f"plq{th}_{i}", [1, 512], F32))
                        for i in range(2)]
                    b = snap()

                    with nc.Block() as blk:
                        @blk.scalar
                        def _(e):
                            barrier(e, b)
                            for q2 in range(2):
                                e.wait_ge(v_a.h, b[v_a] + 32 + q2 + 1)
                                s_a.inc(e.activation(
                                    r2s[0:1, q2 * 512:(q2 + 1) * 512],
                                    rtm2[0:1, q2 * 512:(q2 + 1) * 512], AF.Sqrt))

                        @blk.sync
                        def _(e):
                            barrier(e, b)
                            for dg in range(8):
                                for hc in range(44):
                                    n = dg * 44 + hc
                                    if n >= 4:
                                        e.wait_ge(t_b.h, b[t_b] + n - 3)
                                    r_w3.issue(e, th * 352 + n,
                                               out=w3t[n % 4][:],
                                               in_=w3q[dg * 256:(dg + 1) * 256,
                                                       hc * 128:(hc + 1) * 128],
                                               transpose=True)
                                for u4 in range(4):
                                    u = dg * 4 + u4
                                    dc = dg * 2 + u4 // 2
                                    q2 = u4 % 2
                                    if u >= 2:
                                        e.wait_ge(v_a.h, b[v_a] + u - 1)
                                    r_yl.issue(
                                        e, th * 32 + u,
                                        out=y32l[u % 2][:],
                                        in_=yscr[dc, :, toff + q2 * 512:
                                                 toff + (q2 + 1) * 512])
                                for u4 in range(4):
                                    u = dg * 4 + u4
                                    dc = dg * 2 + u4 // 2
                                    q2 = u4 % 2
                                    e.wait_ge(v_a.h, b[v_a] + u + 1)
                                    r_r2s.issue(
                                        e, th * 32 + u,
                                        out=r2scr[dc, :, toff + q2 * 512:
                                                  toff + (q2 + 1) * 512],
                                        in_=r2sb[u % 2][:])

                        @blk.tensor
                        def _(e):
                            barrier(e, b)
                            for dg in range(8):
                                for hc in range(44):
                                    n = dg * 44 + hc
                                    r_w3.wait(e, th * 352 + n)
                                    if hc == 0 and dg >= 1:
                                        e.wait_ge(v_a.h, b[v_a] + dg * 4)
                                    for dcw in range(2):
                                        for q2 in range(2):
                                            mm = e.matmul(
                                                px3[dcw][:, q2 * 512:(q2 + 1) * 512],
                                                w3t[n % 4][:, dcw * 128:(dcw + 1) * 128],
                                                hidden[:, hc * 1024 + q2 * 512:
                                                       hc * 1024 + (q2 + 1) * 512],
                                                start=(hc == 0), stop=(hc == 43))
                                    t_b.inc(mm)
                                for u4 in range(4):
                                    u = dg * 4 + u4
                                    dcw = u4 // 2
                                    q2 = u4 % 2
                                    e.wait_ge(v_a.h, b[v_a] + u + 1)
                                    e.matmul(pls[q2][0:1, :], ones_fc[:, 0:1],
                                             r2sb[u % 2][:],
                                             start=(dg == 0 and dcw == 0),
                                             stop=(dg == 7 and dcw == 1))
                                    mm = e.matmul(plq[q2][0:1, :], ones_fc[:, 0:1],
                                                  r2sq[u % 2][:],
                                                  start=(dg == 0 and dcw == 0),
                                                  stop=(dg == 7 and dcw == 1))
                                    t_c.inc(mm)

                        @blk.vector
                        def _(e):
                            barrier(e, b)
                            for dg in range(8):
                                for u4 in range(4):
                                    u = dg * 4 + u4
                                    dcw = u4 // 2
                                    q2 = u4 % 2
                                    e.wait_ge(t_b.h, b[t_b] + (dg + 1) * 44)
                                    r_yl.wait(e, th * 32 + u)
                                    if u >= 2:
                                        e.wait_ge(t_c.h, b[t_c] + u - 1)
                                        r_r2s.wait(e, th * 32 + u - 2)
                                    vsync(e, e.tensor_tensor(
                                        r2sb[u % 2][:],
                                        px3[dcw][:, q2 * 512:(q2 + 1) * 512],
                                        y32l[u % 2][:], ALU.add))
                                    v_a.inc(e.tensor_tensor(r2sq[u % 2][:],
                                                            r2sb[u % 2][:],
                                                            r2sb[u % 2][:],
                                                            ALU.mult))
                            for q2 in range(2):
                                e.wait_ge(t_c.h, b[t_c] + 32)
                                vsync(e, e.tensor_scalar(
                                    m2s[0:1, q2 * 512:(q2 + 1) * 512],
                                    pls[q2][0:1, :], 1.0 / D, None, ALU.mult))
                                vsync(e, e.tensor_scalar(tms[0:1, :],
                                                         plq[q2][0:1, :],
                                                         1.0 / D, None, ALU.mult))
                                vsync(e, e.tensor_tensor(
                                    tmm[0:1, :],
                                    m2s[0:1, q2 * 512:(q2 + 1) * 512],
                                    m2s[0:1, q2 * 512:(q2 + 1) * 512], ALU.mult))
                                vsync(e, e.tensor_tensor(tmn[0:1, :], tms[0:1, :],
                                                         tmm[0:1, :], ALU.subtract))
                                vsync(e, e.tensor_scalar(tmo[0:1, :], tmn[0:1, :],
                                                         EPS_LN, None, ALU.add))
                                v_a.inc(e.reciprocal(
                                    rtm2[0:1, q2 * 512:(q2 + 1) * 512],
                                    tmo[0:1, :]))

                    t_b.n = b[t_b] + 8 * 44
                    t_c.n = b[t_c] + 32
                    v_a.n = b[v_a] + 34
                    s_a.n = b[s_a] + 2

                if _FF_PARTS < 3:
                    th_es.close()
                    continue
                with ExitStack() as ph:
                    def psb(name, shape, dt):
                        return ph.enter_context(nc.sbuf_tensor(name, shape, dt, side="right"))
                    r2l = [psb(f"r2l{th}_{i}", [128, 512], F32) for i in range(2)]
                    ot1 = [psb(f"ot1{th}_{i}", [128, 512], F32) for i in range(2)]
                    ot2 = [psb(f"ot2{th}_{i}", [128, 512], F32) for i in range(2)]
                    pmb2 = ph.enter_context(
                        nc.psum_tensor(f"pmb2{th}", [128, 512], F32))
                    prb2 = ph.enter_context(
                        nc.psum_tensor(f"prb2{th}", [128, 512], F32))
                    b = snap()

                    with nc.Block() as blk:
                        @blk.sync
                        def _(e):
                            barrier(e, b)
                            tgt = out if _NORM_TARGET_OUT else yscr
                            for u in range(32):
                                q2, dc = u // 16, u % 16
                                if u >= 2:
                                    e.wait_ge(v_b.h, b[v_b] + u - 1)
                                r_rl.issue(e, th * 32 + u,
                                           out=r2l[u % 2][:],
                                           in_=r2scr[dc, :, toff + q2 * 512:
                                                     toff + (q2 + 1) * 512])
                                if u >= 2 and _NORM_STORES:
                                    up = u - 2
                                    pq2, pdc = up // 16, up % 16
                                    r_out.issue(e, th * 32 + up,
                                                out=tgt[pdc, :, toff + pq2 * 512:
                                                        toff + (pq2 + 1) * 512],
                                                in_=ot2[up % 2][:])
                            if _NORM_STORES:
                                for up in (30, 31):
                                    e.wait_ge(v_b.h, b[v_b] + up + 1)
                                    pq2, pdc = up // 16, up % 16
                                    r_out.issue(e, th * 32 + up,
                                                out=tgt[pdc, :, toff + pq2 * 512:
                                                        toff + (pq2 + 1) * 512],
                                                in_=ot2[up % 2][:])
                            if th == 1 and _NORM_STORES:
                                e.wait_ge(r_out.c[0].h, r_out.c[0].n)
                                e.wait_ge(r_out.c[1].h, r_out.c[1].n)

                        @blk.tensor
                        def _(e):
                            barrier(e, b)
                            for q2 in range(2):
                                if not _NORM_BCAST:
                                    t_c.inc(e.nop())
                                    continue
                                if q2 >= 1:
                                    e.wait_ge(v_b.h, b[v_b] + 16)
                                e.matmul(pmb2[:], ones_fr[0:1, :],
                                         m2s[0:1, q2 * 512:(q2 + 1) * 512],
                                         start=True, stop=True)
                                mm = e.matmul(prb2[:], ones_fr[0:1, :],
                                              r2s[0:1, q2 * 512:(q2 + 1) * 512],
                                              start=True, stop=True)
                                t_c.inc(mm)

                        @blk.vector
                        def _(e):
                            barrier(e, b)
                            for q2 in range(2):
                                for dc in range(NCK):
                                    u = q2 * 16 + dc
                                    e.wait_ge(t_c.h, b[t_c] + q2 + 1)
                                    r_rl.wait(e, th * 32 + u)
                                    if u >= 2 and _NORM_STORES:
                                        r_out.wait(e, th * 32 + u - 2)
                                    if _NORM_COMPUTE:
                                        vsync(e, e.tensor_tensor(ot1[u % 2][:],
                                                                 r2l[u % 2][:],
                                                                 pmb2[:], ALU.subtract))
                                        v_b.inc(e.tensor_tensor(ot2[u % 2][:],
                                                                ot1[u % 2][:],
                                                                prb2[:], ALU.mult))
                                    else:
                                        v_b.inc(e.tensor_copy(ot2[u % 2][:],
                                                              r2l[u % 2][:]))
                    t_c.n = b[t_c] + 2
                    v_b.n = b[v_b] + 32
                th_es.close()

    return nc


def kernel(L0_attn_x, L0_attn_w_qkv, L0_attn_b_qkv, L0_attn_w_proj,
           L0_attn_b_proj, L0_ffn_w1, L0_ffn_w2, L0_ffn_w3):
    x2d = np.ascontiguousarray(
        np.asarray(L0_attn_x, dtype=np.float32).reshape(2 * LK, D))
    wqkv = np.ascontiguousarray(np.asarray(L0_attn_w_qkv, dtype=np.float32))
    bqkv = np.asarray(L0_attn_b_qkv, dtype=np.float32)
    wproj = np.ascontiguousarray(np.asarray(L0_attn_w_proj, dtype=np.float32))
    bproj = np.asarray(L0_attn_b_proj, dtype=np.float32)
    w1 = np.ascontiguousarray(np.asarray(L0_ffn_w1, dtype=np.float32))
    w2 = np.ascontiguousarray(np.asarray(L0_ffn_w2, dtype=np.float32))
    w3 = np.ascontiguousarray(np.asarray(L0_ffn_w3, dtype=np.float32))

    bq_col = np.ascontiguousarray(bqkv.reshape(48, 128).T)
    bq_row = np.ascontiguousarray(bqkv.reshape(1, 3 * D))
    bp_col = np.ascontiguousarray(bproj.reshape(16, 128).T)

    nc = build()
    in_maps = []
    for c in range(8):
        be, qslot = c // 4, c % 4
        xbat = x2d[be * LK:(be + 1) * LK]
        xbc = np.ascontiguousarray(xbat.T).reshape(NCK, 128, LK)
        xmine = xbat[qslot * TOK:(qslot + 1) * TOK]
        xmc = np.ascontiguousarray(xmine.T).reshape(NCK, 128, TOK)
        in_maps.append({
            "xb": xbc, "xm": xmc, "wqkv": wqkv, "bq_col": bq_col,
            "bq_row": bq_row, "wproj": wproj, "bp_col": bp_col,
            "w1": w1, "w2": w2, "w3": w3,
        })
    res = run_bass_kernel_spmd(nc, in_maps, core_ids=list(range(8)))
    outp = np.empty((2 * LK, D), dtype=np.float32)
    for c in range(8):
        be, qslot = c // 4, c % 4
        o = np.asarray(res.results[c]["out"]).reshape(D, TOK)
        outp[be * LK + qslot * TOK: be * LK + (qslot + 1) * TOK] = o.T
    return outp.reshape(2, LK, D)


# version 13
# speedup vs baseline: 1.0274x; 1.0274x over previous
import numpy as np
from contextlib import ExitStack

import concourse.bass as bass
import concourse.mybir as mybir
from concourse.bass_utils import run_bass_kernel_spmd

F32 = mybir.dt.float32
BF16 = mybir.dt.bfloat16
AF = mybir.ActivationFunctionType
ALU = mybir.AluOpType
AX = mybir.AxisListType.X

D = 2048
LK = 8192
TOK = 2048
H = 16
NCK = 16
HFF = 5632
JCK = 44
EPS_LN = 1e-5
EPS_Q = 1e-8

SCALE = float(np.float32(1.0) / np.float32(np.sqrt(np.float32(128.0))))
RS = float(np.sqrt(SCALE))

_KGS = 8
_DO_V = True
_DO_NORM = True
_FF_PARTS = 3
_FF_TH = 2
_NORM_STORES = True
_NORM_COMPUTE = True
_NORM_BCAST = True
_NORM_TARGET_OUT = True
_SKIP_MID = False


class C:
    def __init__(self, h):
        self.h = h
        self.n = 0

    def inc(self, instr, k=1):
        instr.then_inc(self.h, k)
        self.n += k


class DmaRing:

    def __init__(self, nc, es, sems, name, n=2, k=1):
        self.c = [C(es.enter_context(nc.semaphore(f"{name}{i}")))
                  for i in range(n)]
        for cc in self.c:
            sems.append(cc)
        self.n = n
        self.k = k

    def issue(self, e, slot, out, in_, **kw):
        cc = self.c[slot % self.n]
        e.dma_start(out=out, in_=in_, **kw).then_inc(cc.h, 16)
        cc.n += 16

    def wait(self, e, slot):
        e.wait_ge(self.c[slot % self.n].h, (slot // self.n + 1) * self.k * 16)


def build(upto=7, debug=False):
    nc = bass.Bass("TRN2", target_bir_lowering=False)
    sck = "ExternalOutput" if debug else "Internal"

    xb = nc.dram_tensor("xb", [NCK, 128, LK], F32, kind="ExternalInput").ap()
    xm = nc.dram_tensor("xm", [NCK, 128, TOK], F32, kind="ExternalInput").ap()
    wqkv = nc.dram_tensor("wqkv", [D, 3 * D], F32, kind="ExternalInput").ap()
    bq_col = nc.dram_tensor("bq_col", [128, 48], F32, kind="ExternalInput").ap()
    bq_row = nc.dram_tensor("bq_row", [1, 3 * D], F32, kind="ExternalInput").ap()
    wproj = nc.dram_tensor("wproj", [D, D], F32, kind="ExternalInput").ap()
    bp_col = nc.dram_tensor("bp_col", [128, 16], F32, kind="ExternalInput").ap()
    w1 = nc.dram_tensor("w1", [HFF, D], F32, kind="ExternalInput").ap()
    w2 = nc.dram_tensor("w2", [HFF, D], F32, kind="ExternalInput").ap()
    w3 = nc.dram_tensor("w3", [D, HFF], F32, kind="ExternalInput").ap()
    out = nc.dram_tensor("out", [NCK, 128, TOK], F32, kind="ExternalOutput").ap()

    kscr = nc.dram_tensor("kscr", [H, 128, LK], BF16, kind=sck).ap()
    vscr = nc.dram_tensor("vscr", [H, LK, 128], BF16, kind=sck).ap()
    w1q = nc.dram_tensor("w1qs", [HFF, D], BF16, kind=sck).ap()
    w2q = nc.dram_tensor("w2qs", [HFF, D], BF16, kind=sck).ap()
    w3q = nc.dram_tensor("w3qs", [D, HFF], BF16, kind=sck).ap()
    yscr = nc.dram_tensor("yscr", [NCK, 128, TOK], F32, kind=sck).ap()
    r2scr = nc.dram_tensor("r2scr", [NCK, 128, TOK], F32, kind=sck).ap()
    if debug:
        qdump = nc.dram_tensor("qdump", [128, H * TOK], BF16,
                               kind="ExternalOutput").ap()
        ndump = nc.dram_tensor("ndump", [128, 32], F32,
                               kind="ExternalOutput").ap()
        n2dump = nc.dram_tensor("n2dump", [1, 1024], F32,
                                kind="ExternalOutput").ap()
        ydump = nc.dram_tensor("ydump", [128, NCK * TOK], BF16,
                               kind="ExternalOutput").ap()

    es = ExitStack()
    with es:
        def sb(name, shape, dt):
            return es.enter_context(nc.sbuf_tensor(name, shape, dt, side="left"))

        ones_bf = sb("ones_bf", [128, 1], BF16)
        ones_fc = sb("ones_fc", [128, 1], F32)
        ones_fr = sb("ones_fr", [64, 128], F32)
        bqc = sb("bqc", [128, 48], F32)
        bpc = sb("bpc", [128, 16], F32)
        bqrs = sb("bqrs", [128, 16], F32)
        bkrs = sb("bkrs", [128, 16], F32)
        bvb_es = ExitStack()
        bvb = bvb_es.enter_context(nc.sbuf_tensor("bvb", [128, D], F32,
                                                  side="right"))
        knmax = sb("knmax", [1, 256], F32)
        qnmax = sb("qnmax", [1, 64], F32)
        negc = sb("negc", [128, 16], F32)

        qg = C(es.enter_context(nc.semaphore("qg")))
        qs = C(es.enter_context(nc.semaphore("qs")))
        qa = C(es.enter_context(nc.semaphore("qa")))
        t_a = C(es.enter_context(nc.semaphore("t_a")))
        t_b = C(es.enter_context(nc.semaphore("t_b")))
        t_c = C(es.enter_context(nc.semaphore("t_c")))
        s_a = C(es.enter_context(nc.semaphore("s_a")))
        v_a = C(es.enter_context(nc.semaphore("v_a")))
        v_b = C(es.enter_context(nc.semaphore("v_b")))
        g_a = C(es.enter_context(nc.semaphore("g_a")))
        v_s = C(es.enter_context(nc.semaphore("v_s")))
        v_q = C(es.enter_context(nc.semaphore("v_q")))
        v_q2 = C(es.enter_context(nc.semaphore("v_q2")))
        t_q = C(es.enter_context(nc.semaphore("t_q")))
        sems = [qg, qs, qa, t_a, t_b, t_c, s_a, v_a, v_b, g_a, v_q, v_q2, t_q]

        def vsync(e, instr):
            v_s.inc(instr)
            e.wait_ge(v_s.h, v_s.n)

        def snap():
            return {s: s.n for s in sems}

        def barrier(e, b):
            for s, n in b.items():
                if n:
                    e.wait_ge(s.h, n)

        b = snap()
        with nc.Block() as blk:
            @blk.sync
            def _(e):
                e.dma_start(out=bqc[:], in_=bq_col[:]).then_inc(qs.h, 16)
                e.dma_start(out=bpc[:], in_=bp_col[:]).then_inc(qs.h, 16)
                e.dma_start(out=bvb[0:1, :], in_=bq_row[0:1, 4096:6144]) \
                    .then_inc(qs.h, 16)

            @blk.gpsimd
            def _(e):
                e.memset(ones_bf[:], 1.0)
                e.memset(ones_fc[:], 1.0)
                e.memset(ones_fr[:], 1.0)
                g_a.inc(e.nop())

            @blk.vector
            def _(e):
                e.wait_ge(qs.h, b[qs] + 48)
                e.tensor_scalar(bqrs[:], bqc[:, 0:16], RS, None, ALU.mult)
                v_a.inc(e.tensor_scalar(bkrs[:], bqc[:, 16:32], RS, None,
                                        ALU.mult))
        qs.n += 48

        with nc.Block() as blk, \
                nc.psum_tensor("pstp", [128, 512], F32) as pstp:
            @blk.tensor
            def _(e):
                e.wait_ge(qs.h, b[qs] + 48)
                e.wait_ge(g_a.h, b[g_a] + 1)
                for i in range(4):
                    if i > 0:
                        e.wait_ge(v_a.h, b[v_a] + 1 + i)
                    t_a.inc(e.matmul(pstp[:, 0:512], ones_fr[0:1, :],
                                     bvb[0:1, i * 512:(i + 1) * 512],
                                     start=True, stop=True))

            @blk.vector
            def _(e):
                for i in range(4):
                    e.wait_ge(t_a.h, b[t_a] + i + 1)
                    v_a.inc(e.tensor_copy(bvb[:, i * 512:(i + 1) * 512],
                                          pstp[:, 0:512]))

        QCH = []
        for wi_, (ws_, wd_, nrb_, W_) in enumerate(
                ((w1, w1q, 44, 2048), (w2, w2q, 44, 2048),
                 (w3, w3q, 16, 5632))):
            for rb_ in range(nrb_):
                for co_ in range(0, W_, 2048):
                    QCH.append((ws_, wd_, rb_, co_, min(2048, W_ - co_), wi_))
        NQ = len(QCH)
        racc = sb("racc", [128, NQ], F32)
        gi3 = sb("gi3", [128, 3], F32)
        xm16_es = ExitStack()
        xm16 = xm16_es.enter_context(nc.sbuf_tensor("xm16", [128, NCK * TOK],
                                                    BF16, side="left"))
        r_xm = DmaRing(nc, es, sems, "r_xm", n=1, k=16)
        if upto >= 2 and not _SKIP_MID:
            with ExitStack() as ph:
                def psb(name, shape, dt):
                    return ph.enter_context(nc.sbuf_tensor(name, shape, dt, side="right"))
                wkv = psb("wkv", [128, NCK * 2048], BF16)
                xg = [psb(f"xg{i}", [128, NCK * 512], BF16) for i in range(2)]
                ksb = [psb(f"ksb{i}", [128, 512], BF16) for i in range(2)]
                sqk = [psb(f"sqk{i}", [128, 512], BF16) for i in range(2)]
                vsb = [psb(f"vsb{i}", [128, 512], BF16) for i in range(2)]
                pk = [ph.enter_context(nc.psum_tensor(f"pk{i}", [128, 512], F32))
                      for i in range(2)]
                pv = [ph.enter_context(nc.psum_tensor(f"pv{i}", [128, 512], F32))
                      for i in range(2)]
                pn = [ph.enter_context(nc.psum_tensor(f"pn{i}", [1, 512], F32))
                      for i in range(2)]
                r_wkv = DmaRing(nc, es, sems, "r_wkv", n=1, k=32)
                r_xg = DmaRing(nc, es, sems, "r_xg", n=2, k=16)
                r_kst = DmaRing(nc, es, sems, "r_kst", n=2, k=1)
                r_vst = DmaRing(nc, es, sems, "r_vst", n=2, k=4)
                b = snap()

                with nc.Block() as blk:
                    @blk.gpsimd
                    def _(e):
                        barrier(e, b)
                        for hp in range(2):
                            if hp == 1:
                                e.wait_ge(t_a.h, b[t_a] + 128)
                                e.wait_ge(t_c.h, b[t_c] + 128)
                            for dc in range(NCK):
                                r_wkv.issue(e, hp,
                                            out=wkv[:, dc * 2048:dc * 2048 + 1024],
                                            in_=wqkv[dc * 128:(dc + 1) * 128,
                                                     2048 + hp * 1024:3072 + hp * 1024])
                                r_wkv.issue(e, hp,
                                            out=wkv[:, dc * 2048 + 1024:(dc + 1) * 2048],
                                            in_=wqkv[dc * 128:(dc + 1) * 128,
                                                     4096 + hp * 1024:5120 + hp * 1024])
                            for kg in range(16):
                                s = hp * 16 + kg
                                if s >= 2:
                                    e.wait_ge(t_a.h, b[t_a] + (s - 1) * 8)
                                    e.wait_ge(t_c.h, b[t_c] + (s - 1) * 8)
                                for dc in range(NCK):
                                    r_xg.issue(e, s,
                                               out=xg[s % 2][:, dc * 512:(dc + 1) * 512],
                                               in_=xb[dc, :, kg * 512:(kg + 1) * 512])
                        for dc in range(NCK):
                            r_xm.issue(e, 0, out=xm16[:, dc * TOK:(dc + 1) * TOK],
                                       in_=xm[dc, :, :])
                    @blk.tensor
                    def _(e):
                        barrier(e, b)
                        for hp in range(2):
                            r_wkv.wait(e, hp)
                            for kg in range(16):
                                s = hp * 16 + kg
                                r_xg.wait(e, s)
                                for i in range(8):
                                    idx = s * 8 + i
                                    h_l = i
                                    if idx >= 2:
                                        e.wait_ge(s_a.h, b[s_a] + idx - 1)
                                    for dc in range(NCK):
                                        mm = e.matmul(
                                            pk[idx % 2][:],
                                            wkv[:, dc * 2048 + h_l * 128:dc * 2048 + (h_l + 1) * 128],
                                            xg[s % 2][:, dc * 512:(dc + 1) * 512],
                                            start=(dc == 0), stop=(dc == 15))
                                    t_a.inc(mm)
                                    if i >= 1:
                                        p = idx - 1
                                        e.wait_ge(s_a.h, b[s_a] + p + 1)
                                        if p >= 2:
                                            e.wait_ge(v_a.h, b[v_a] + p - 1)
                                        t_b.inc(e.matmul(pn[p % 2][0:1, :],
                                                         ones_bf[:, 0:1],
                                                         sqk[p % 2][:],
                                                         start=True, stop=True))
                                p = s * 8 + 7
                                e.wait_ge(s_a.h, b[s_a] + p + 1)
                                if p >= 2:
                                    e.wait_ge(v_a.h, b[v_a] + p - 1)
                                t_b.inc(e.matmul(pn[p % 2][0:1, :],
                                                 ones_bf[:, 0:1], sqk[p % 2][:],
                                                 start=True, stop=True))
                                for j in range(8):
                                    jdx = s * 8 + j
                                    g4l, kc = j // 4, j % 4
                                    if jdx >= 2:
                                        e.wait_ge(v_b.h, b[v_b] + jdx - 1)
                                    for dc in range(NCK):
                                        mm = e.matmul(
                                            pv[jdx % 2][:],
                                            xg[s % 2][:, dc * 512 + kc * 128:dc * 512 + (kc + 1) * 128],
                                            wkv[:, dc * 2048 + 1024 + g4l * 512:dc * 2048 + 1024 + (g4l + 1) * 512],
                                            start=(dc == 0), stop=(dc == 15))
                                    t_c.inc(mm)

                    @blk.scalar
                    def _(e):
                        barrier(e, b)
                        for idx in range(256):
                            s = idx // 8
                            i = idx % 8
                            h = (s // 16) * 8 + i
                            e.wait_ge(t_a.h, b[t_a] + idx + 1)
                            if idx >= 2:
                                r_kst.wait(e, idx - 2)
                                e.wait_ge(t_b.h, b[t_b] + idx - 1)
                            e.activation(ksb[idx % 2][:], pk[idx % 2][:],
                                         AF.Identity, bias=bkrs[:, h:h + 1],
                                         scale=RS)
                            s_a.inc(e.activation(sqk[idx % 2][:],
                                                 pk[idx % 2][:], AF.Square,
                                                 bias=bkrs[:, h:h + 1],
                                                 scale=RS))

                    @blk.vector
                    def _(e):
                        barrier(e, b)
                        for hp in range(2):
                            for kg in range(16):
                                s = hp * 16 + kg
                                for i in range(8):
                                    idx = s * 8 + i
                                    h = hp * 8 + i
                                    col = h * 16 + kg
                                    e.wait_ge(t_b.h, b[t_b] + idx + 1)
                                    rm = e.reduce_max(knmax[0:1, col:col + 1],
                                                      pn[idx % 2][0:1, :],
                                                      axis=AX)
                                    vsync(e, rm)
                                    v_a.inc(e.nop())
                                for j in range(8):
                                    jdx = s * 8 + j
                                    g4l = j // 4
                                    e.wait_ge(t_c.h, b[t_c] + jdx + 1)
                                    if jdx >= 2:
                                        r_vst.wait(e, jdx - 2)
                                    v_b.inc(e.tensor_tensor(
                                        vsb[jdx % 2][:], pv[jdx % 2][:],
                                        bvb[:, (hp * 2 + g4l) * 512:(hp * 2 + g4l + 1) * 512],
                                        ALU.add))

                    @blk.sync
                    def _(e):
                        barrier(e, b)
                        for hp in range(2):
                            for kg in range(16):
                                s = hp * 16 + kg
                                for i in range(8):
                                    idx = s * 8 + i
                                    h = hp * 8 + i
                                    e.wait_ge(s_a.h, b[s_a] + idx + 1)
                                    r_kst.issue(
                                        e, idx,
                                        out=kscr[h, :, kg * 512:(kg + 1) * 512],
                                        in_=ksb[idx % 2][:])
                                for j in range(8):
                                    jdx = s * 8 + j
                                    g4l, kc = j // 4, j % 4
                                    g4 = hp * 2 + g4l
                                    e.wait_ge(v_b.h, b[v_b] + jdx + 1)
                                    for hh in range(4):
                                        r_vst.issue(
                                            e, jdx,
                                            out=vscr[g4 * 4 + hh,
                                                     kg * 512 + kc * 128:kg * 512 + (kc + 1) * 128, :],
                                            in_=vsb[jdx % 2][:, hh * 128:(hh + 1) * 128])
                t_a.n = b[t_a] + 256
                t_b.n = b[t_b] + 256
                t_c.n = b[t_c] + 256
                s_a.n = b[s_a] + 256
                v_a.n = b[v_a] + 256
                v_b.n = b[v_b] + 256


        bvb_es.close()

        qt_es = ExitStack()
        qtsb = qt_es.enter_context(nc.sbuf_tensor("qtsb", [128, H * TOK], BF16,
                                                  side="right"))
        if upto >= 3 and not _SKIP_MID:
            with ExitStack() as ph:
                def psb(name, shape, dt):
                    return ph.enter_context(nc.sbuf_tensor(name, shape, dt, side="right"))
                wqh = [psb(f"wqh{i}", [128, NCK * 128], BF16) for i in range(2)]
                sqq = [psb(f"sqq{i}", [128, 512], BF16) for i in range(2)]
                pq = [ph.enter_context(nc.psum_tensor(f"pq{i}", [128, 512], F32))
                      for i in range(2)]
                pnq = [ph.enter_context(nc.psum_tensor(f"pnq{i}", [1, 512], F32))
                       for i in range(2)]
                r_wq = DmaRing(nc, es, sems, "r_wq", n=2, k=1)
                b = snap()

                with nc.Block() as blk:
                    @blk.gpsimd
                    def _(e):
                        barrier(e, b)
                        for h in range(H):
                            if h >= 2:
                                e.wait_ge(t_a.h, b[t_a] + (h - 1) * 4)
                            r_wq.issue(
                                e, h,
                                out=wqh[h % 2][:].rearrange("p (a j) -> p a j", a=NCK),
                                in_=wqkv[:, h * 128:(h + 1) * 128]
                                .rearrange("(a p) j -> p a j", p=128))

                    @blk.tensor
                    def _(e):
                        barrier(e, b)
                        r_xm.wait(e, 0)
                        for h in range(H):
                            r_wq.wait(e, h)
                            for qt in range(4):
                                n = h * 4 + qt
                                if n >= 2:
                                    e.wait_ge(s_a.h, b[s_a] + n - 1)
                                for dc in range(NCK):
                                    mm = e.matmul(
                                        pq[n % 2][:],
                                        wqh[h % 2][:, dc * 128:(dc + 1) * 128],
                                        xm16[:, dc * TOK + qt * 512:dc * TOK + (qt + 1) * 512],
                                        start=(dc == 0), stop=(dc == 15))
                                t_a.inc(mm)
                                if n >= 1:
                                    p = n - 1
                                    e.wait_ge(s_a.h, b[s_a] + p + 1)
                                    if p >= 2:
                                        e.wait_ge(v_a.h, b[v_a] + p - 1)
                                    t_b.inc(e.matmul(pnq[p % 2][0:1, :],
                                                     ones_bf[:, 0:1],
                                                     sqq[p % 2][:],
                                                     start=True, stop=True))
                        e.wait_ge(s_a.h, b[s_a] + 64)
                        e.wait_ge(v_a.h, b[v_a] + 62)
                        t_b.inc(e.matmul(pnq[63 % 2][0:1, :], ones_bf[:, 0:1],
                                         sqq[63 % 2][:], start=True, stop=True))

                    @blk.scalar
                    def _(e):
                        barrier(e, b)
                        for n in range(64):
                            h, qt = n // 4, n % 4
                            e.wait_ge(t_a.h, b[t_a] + n + 1)
                            if n >= 2:
                                e.wait_ge(t_b.h, b[t_b] + n - 1)
                            e.activation(
                                qtsb[:, h * TOK + qt * 512:h * TOK + (qt + 1) * 512],
                                pq[n % 2][:], AF.Identity,
                                bias=bqrs[:, h:h + 1], scale=RS)
                            s_a.inc(e.activation(
                                sqq[n % 2][:], pq[n % 2][:], AF.Square,
                                bias=bqrs[:, h:h + 1], scale=RS))

                    @blk.vector
                    def _(e):
                        barrier(e, b)
                        for n in range(64):
                            e.wait_ge(t_b.h, b[t_b] + n + 1)
                            rm = e.reduce_max(qnmax[0:1, n:n + 1],
                                              pnq[n % 2][0:1, :], axis=AX)
                            vsync(e, rm)
                            v_a.inc(e.nop())
                t_a.n = b[t_a] + 64
                t_b.n = b[t_b] + 64
                s_a.n = b[s_a] + 64
                v_a.n = b[v_a] + 64

            with ExitStack() as ph, \
                    nc.psum_tensor("pnc", [128, 16], F32) as pnc:
                km2 = ph.enter_context(nc.sbuf_tensor("km2", [1, 16], F32, side="right"))
                qm2 = ph.enter_context(nc.sbuf_tensor("qm2", [1, 16], F32, side="right"))
                ncr = ph.enter_context(nc.sbuf_tensor("ncr", [1, 16], F32, side="right"))
                ncs = ph.enter_context(nc.sbuf_tensor("ncs", [1, 16], F32, side="right"))
                b = snap()
                with nc.Block() as blk:
                    @blk.vector
                    def _(e):
                        barrier(e, b)
                        for h in range(H):
                            vsync(e, e.reduce_max(km2[0:1, h:h + 1],
                                                  knmax[0:1, h * 16:(h + 1) * 16],
                                                  axis=AX))
                            vsync(e, e.reduce_max(qm2[0:1, h:h + 1],
                                                  qnmax[0:1, h * 4:(h + 1) * 4],
                                                  axis=AX))
                        vsync(e, e.tensor_tensor(ncr[0:1, :], km2[0:1, :],
                                                 qm2[0:1, :], ALU.add))
                        v_a.inc(e.tensor_scalar(ncs[0:1, :], ncr[0:1, :], -0.5,
                                                None, ALU.mult))
                        e.wait_ge(t_a.h, b[t_a] + 1)
                        v_a.inc(e.tensor_copy(negc[:], pnc[:, 0:16]))

                    @blk.tensor
                    def _(e):
                        e.wait_ge(v_a.h, b[v_a] + 1)
                        t_a.inc(e.matmul(pnc[:, 0:16], ones_fr[0:1, :],
                                         ncs[0:1, 0:16], start=True, stop=True))

                    if debug:
                        @blk.sync
                        def _(e):
                            e.wait_ge(v_a.h, b[v_a] + 2)
                            e.dma_start(out=n2dump[0:1, 320:336], in_=km2[0:1, :]) \
                                .then_inc(qs.h, 16)
                            e.dma_start(out=n2dump[0:1, 336:352], in_=qm2[0:1, :]) \
                                .then_inc(qs.h, 16)
                            e.dma_start(out=n2dump[0:1, 352:368], in_=ncs[0:1, :]) \
                                .then_inc(qs.h, 16)
                            e.dma_start(out=n2dump[0:1, 368:384], in_=ncr[0:1, :]) \
                                .then_inc(qs.h, 16)
                        qs.n += 64

            if debug and upto == 3:
                b = snap()
                with nc.Block() as blk:
                    @blk.sync
                    def _(e):
                        barrier(e, b)
                        e.dma_start(out=qdump[:], in_=qtsb[:]).then_inc(qs.h, 16)
                        e.dma_start(out=ndump[:, 0:16], in_=negc[:]) \
                            .then_inc(qs.h, 16)
                        e.dma_start(out=n2dump[0:1, 0:256], in_=knmax[0:1, :]) \
                            .then_inc(qs.h, 16)
                        e.dma_start(out=n2dump[0:1, 256:320], in_=qnmax[0:1, :]) \
                            .then_inc(qs.h, 16)
                        e.wait_ge(qs.h, b[qs] + 64)
                qs.n += 64


        xm16_es.close()

        if upto >= 4 and not _SKIP_MID:
            with ExitStack() as ph:
                def psb(name, shape, dt):
                    return ph.enter_context(nc.sbuf_tensor(name, shape, dt, side="right"))
                kh = [psb(f"kh{i}", [128, LK], BF16) for i in range(2)]
                vh = [psb(f"vh{i}", [128, LK], BF16) for i in range(2)]
                exs = [psb(f"exs{i}", [128, 1024], BF16) for i in range(4)]
                rlsb = psb("rlsb", [64, 512], F32)
                rbsb = psb("rbsb", [128, 1024], F32)
                ps_s = [ph.enter_context(nc.psum_tensor(f"ps_s{i}", [128, 1024], F32))
                        for i in range(2)]
                po = [ph.enter_context(nc.psum_tensor(f"po{i}", [128, 512], F32))
                      for i in range(2)]
                pl = [ph.enter_context(nc.psum_tensor(f"pl{i}", [128, 512], F32))
                      for i in range(2)]
                wbuf2 = [psb(f"wbuf2_{i}", [128, 2048], F32) for i in range(3)]
                tq1 = psb("tq1", [128, 2048], BF16)
                tq2 = psb("tq2", [128, 2048], BF16)
                wqb2 = [psb(f"wqb2_{i}", [128, 2048], BF16) for i in range(2)]
                rtot3 = psb("rtot3", [128, 3], F32)
                gsc3 = psb("gsc3", [1, 3], F32)
                grc3 = psb("grc3", [1, 3], F32)
                r_kv = DmaRing(nc, es, sems, "r_kv", n=2, k=2)
                r_q1 = DmaRing(nc, es, sems, "r_q1", n=3, k=1)
                r_q2 = DmaRing(nc, es, sems, "r_q2", n=3, k=1)
                r_q2s = DmaRing(nc, es, sems, "r_q2s", n=2, k=1)
                NW = 44 * 128 * 2048
                _q1c = [23, 23, 23, 23, 22, 22] + [0] * 10
                q1_of_h = []
                _qa = 0
                for h_ in range(H):
                    q1_of_h.append(list(range(_qa, _qa + _q1c[h_])))
                    _qa += _q1c[h_]
                q2_of_h = [list(range((h_ - 7) * 17, (h_ - 6) * 17))
                           if 7 <= h_ <= 14 else [] for h_ in range(H)]
                b = snap()

                with nc.Block() as blk:
                    @blk.sync
                    def _(e):
                        barrier(e, b)
                        for h in range(H):
                            if h >= 2:
                                e.wait_ge(t_a.h, b[t_a] + (h - 1) * 128)
                                e.wait_ge(t_b.h, b[t_b] + (h - 1) * 128)
                            r_kv.issue(e, h, out=kh[h % 2][:], in_=kscr[h, :, :])
                            r_kv.issue(
                                e, h,
                                out=vh[h % 2][:].rearrange("p (a d) -> p a d", a=64),
                                in_=vscr[h].rearrange("(a p) d -> p a d", p=128))

                    @blk.gpsimd
                    def _(e):
                        barrier(e, b)
                        for ci in range(NQ):
                            ws_, wd_, rb_, co_, wdt_, wi_ = QCH[ci]
                            if ci >= 3:
                                e.wait_ge(v_q.h, b[v_q] + ci - 2)
                            r_q1.issue(e, ci, out=wbuf2[ci % 3][:, 0:wdt_],
                                       in_=ws_[rb_ * 128:(rb_ + 1) * 128,
                                               co_:co_ + wdt_])
                        e.wait_ge(v_q.h, b[v_q] + NQ)
                        for ci in range(NQ):
                            ws_, wd_, rb_, co_, wdt_, wi_ = QCH[ci]
                            if ci >= 3:
                                e.wait_ge(v_q2.h, b[v_q2] + ci - 2)
                            r_q2.issue(e, ci, out=wbuf2[ci % 3][:, 0:wdt_],
                                       in_=ws_[rb_ * 128:(rb_ + 1) * 128,
                                               co_:co_ + wdt_])
                            if ci >= 2:
                                up = ci - 2
                                us_, ud_, urb, uco, uwd, uwi = QCH[up]
                                e.wait_ge(v_q2.h, b[v_q2] + up + 1)
                                r_q2s.issue(e, up,
                                            out=ud_[urb * 128:(urb + 1) * 128,
                                                    uco:uco + uwd],
                                            in_=wqb2[up % 2][:, 0:uwd])
                        e.wait_ge(v_q2.h, b[v_q2] + NQ)
                        for up in (NQ - 2, NQ - 1):
                            us_, ud_, urb, uco, uwd, uwi = QCH[up]
                            r_q2s.issue(e, up,
                                        out=ud_[urb * 128:(urb + 1) * 128,
                                                uco:uco + uwd],
                                        in_=wqb2[up % 2][:, 0:uwd])

                    @blk.tensor
                    def _(e):
                        barrier(e, b)
                        for h in range(H):
                            if h == 7:
                                e.wait_ge(v_q.h, b[v_q] + NQ + 3)
                            for qp in range(2):
                                g = h * 2 + qp
                                eb = b[s_a] + g * 64
                                if qp == 0:
                                    r_kv.wait(e, h)
                                for kc in range(64):
                                    for q2 in range(2):
                                        mm = e.matmul(
                                            ps_s[kc % 2][:, q2 * 512:(q2 + 1) * 512],
                                            kh[h % 2][:, kc * 128:(kc + 1) * 128],
                                            qtsb[:, h * TOK + qp * 1024 + q2 * 512:
                                                 h * TOK + qp * 1024 + (q2 + 1) * 512],
                                            start=True, stop=True)
                                    t_a.inc(mm)
                                    if kc == 1 and g >= 1:
                                        if g != 14:
                                            e.wait_ge(v_a.h, b[v_a] + g)
                                            e.matmul(
                                                pl[1][:], ones_fr[0:1, :],
                                                rlsb[0:1, :], start=True,
                                                stop=True,
                                                tile_position=(0, 0))
                                            mm = e.matmul(
                                                pl[0][:], ones_fr[32:33, :],
                                                rlsb[32:33, :], start=True,
                                                stop=True,
                                                tile_position=(32, 0))
                                            t_c.inc(mm)
                                        e.wait_ge(v_b.h, b[v_b] + g)
                                    if kc >= 1:
                                        p = kc - 1
                                        e.wait_ge(s_a.h, eb + p + 1)
                                        for q2 in range(2):
                                            e.matmul(po[q2][:],
                                                     vh[h % 2][:, p * 128:(p + 1) * 128],
                                                     exs[p % 4][:, q2 * 512:(q2 + 1) * 512],
                                                     start=(p == 0), stop=False)
                                        for q2 in range(2):
                                            mm = e.matmul(pl[0][q2 * 32:q2 * 32 + 1, :],
                                                          ones_bf[:, 0:1],
                                                          exs[p % 4][:, q2 * 512:(q2 + 1) * 512],
                                                          start=(p == 0), stop=False,
                                                          tile_position=(0, q2 * 32))
                                        t_b.inc(mm)
                                e.wait_ge(s_a.h, eb + 64)
                                for q2 in range(2):
                                    e.matmul(po[q2][:], vh[h % 2][:, 63 * 128:64 * 128],
                                             exs[63 % 4][:, q2 * 512:(q2 + 1) * 512],
                                             start=False, stop=True)
                                for q2 in range(2):
                                    mm = e.matmul(pl[0][q2 * 32:q2 * 32 + 1, :],
                                                  ones_bf[:, 0:1],
                                                  exs[63 % 4][:, q2 * 512:(q2 + 1) * 512],
                                                  start=False, stop=True,
                                                  tile_position=(0, q2 * 32))
                                t_b.inc(mm)
                            if h == 6:
                                e.wait_ge(v_a.h, b[v_a] + 14)
                                e.matmul(pl[1][:], ones_fr[0:1, :],
                                         rlsb[0:1, :], start=True, stop=True,
                                         tile_position=(0, 0))
                                mm = e.matmul(pl[0][:], ones_fr[32:33, :],
                                              rlsb[32:33, :], start=True,
                                              stop=True,
                                              tile_position=(32, 0))
                                t_c.inc(mm)
                                e.wait_ge(v_q.h, b[v_q] + NQ + 1)
                                t_q.inc(e.matmul(ps_s[0][0:1, 0:3],
                                                 ones_fc[:, 0:1], rtot3[:, 0:3],
                                                 start=True, stop=True))
                                e.wait_ge(v_q.h, b[v_q] + NQ + 2)
                                t_q.inc(e.matmul(ps_s[1][:, 0:3], ones_fr[0:1, :],
                                                 grc3[0:1, 0:3],
                                                 start=True, stop=True))
                        e.wait_ge(v_a.h, b[v_a] + 32)
                        e.matmul(pl[1][:], ones_fr[0:1, :], rlsb[0:1, :],
                                 start=True, stop=True, tile_position=(0, 0))
                        mm = e.matmul(pl[0][:], ones_fr[32:33, :],
                                      rlsb[32:33, :], start=True, stop=True,
                                      tile_position=(32, 0))
                        t_c.inc(mm)

                    @blk.scalar
                    def _(e):
                        barrier(e, b)
                        for h in range(H):
                            for qp in range(2):
                                g = h * 2 + qp
                                for kc in range(64):
                                    e.wait_ge(t_a.h, b[t_a] + g * 64 + kc + 1)
                                    if kc >= 4:
                                        e.wait_ge(t_b.h, b[t_b] + g * 64 + kc - 3)
                                    s_a.inc(e.activation(exs[kc % 4][:],
                                                         ps_s[kc % 2][:], AF.Exp,
                                                         bias=negc[:, h:h + 1]))

                    @blk.vector
                    def _(e):
                        barrier(e, b)
                        for h in range(H):
                            for qp in range(2):
                                g = h * 2 + qp
                                e.wait_ge(t_b.h, b[t_b] + (g + 1) * 64)
                                e.reciprocal(rlsb[0:1, :], pl[0][0:1, :])
                                v_a.inc(e.reciprocal(rlsb[32:33, :],
                                                     pl[0][32:33, :]))
                                e.wait_ge(t_c.h, b[t_c] + g + 1)
                                e.tensor_copy(rbsb[:, 0:512], pl[1][:])
                                vsync(e, e.tensor_copy(rbsb[:, 512:1024],
                                                       pl[0][:]))
                                for q2 in range(2):
                                    col = h * TOK + qp * 1024 + q2 * 512
                                    mm = e.tensor_tensor(
                                        qtsb[:, col:col + 512], po[q2][:],
                                        rbsb[:, q2 * 512:(q2 + 1) * 512],
                                        ALU.mult)
                                v_b.inc(mm)
                                e.wait_ge(v_b.h, v_b.n)
                            if qp == 1:
                                for ci in q1_of_h[h]:
                                    wdt_ = QCH[ci][4]
                                    r_q1.wait(e, ci)
                                    rs = e.reduce_sum(racc[:, ci:ci + 1],
                                                      wbuf2[ci % 3][:, 0:wdt_],
                                                      axis=AX,
                                                      apply_absolute_value=True)
                                    vsync(e, rs)
                                    v_q.inc(e.nop())
                                if h == 6:
                                    vsync(e, e.reduce_sum(rtot3[:, 0:1],
                                                          racc[:, 0:44], axis=AX))
                                    vsync(e, e.reduce_sum(rtot3[:, 1:2],
                                                          racc[:, 44:88], axis=AX))
                                    vsync(e, e.reduce_sum(rtot3[:, 2:3],
                                                          racc[:, 88:NQ], axis=AX))
                                    v_q.inc(e.nop())
                                    e.wait_ge(t_q.h, b[t_q] + 1)
                                    vsync(e, e.tensor_scalar(
                                        gsc3[0:1, 0:3], ps_s[0][0:1, 0:3],
                                        1.0 / NW, EPS_Q, ALU.mult, ALU.add))
                                    vsync(e, e.reciprocal(grc3[0:1, 0:3],
                                                          gsc3[0:1, 0:3]))
                                    v_q.inc(e.nop())
                                    e.wait_ge(t_q.h, b[t_q] + 2)
                                    vsync(e, e.tensor_copy(gi3[:],
                                                           ps_s[1][:, 0:3]))
                                    v_q.inc(e.nop())
                                for ci in q2_of_h[h]:
                                    ws_, wd_, rb_, co_, wdt_, wi_ = QCH[ci]
                                    r_q2.wait(e, ci)
                                    if ci >= 2:
                                        r_q2s.wait(e, ci - 2)
                                    vsync(e, e.tensor_scalar(
                                        tq1[:, 0:wdt_], wbuf2[ci % 3][:, 0:wdt_],
                                        gi3[:, wi_:wi_ + 1], 0.5, ALU.mult,
                                        ALU.is_gt))
                                    vsync(e, e.tensor_scalar(
                                        tq2[:, 0:wdt_], wbuf2[ci % 3][:, 0:wdt_],
                                        gi3[:, wi_:wi_ + 1], -0.5, ALU.mult,
                                        ALU.is_lt))
                                    v_q2.inc(e.tensor_tensor(
                                        wqb2[ci % 2][:, 0:wdt_], tq1[:, 0:wdt_],
                                        tq2[:, 0:wdt_], ALU.subtract))
                                    e.wait_ge(v_q2.h, v_q2.n)
                t_a.n = b[t_a] + 2048
                t_b.n = b[t_b] + 2048
                t_c.n = b[t_c] + 32
                s_a.n = b[s_a] + 2048
                v_a.n = b[v_a] + 32
                v_b.n = b[v_b] + 32
                v_q.n = b[v_q] + NQ + 3
                v_q2.n = b[v_q2] + NQ
                t_q.n = b[t_q] + 2

            if debug:
                b = snap()
                with nc.Block() as blk:
                    @blk.sync
                    def _(e):
                        barrier(e, b)
                        e.dma_start(out=qdump[:], in_=qtsb[:]).then_inc(qs.h, 16)
                        e.dma_start(out=ndump[:, 0:16], in_=negc[:]) \
                            .then_inc(qs.h, 16)
                        e.wait_ge(qs.h, b[qs] + 32)
                qs.n += 32

        if upto >= 5 and not _SKIP_MID:
            with ExitStack() as ph:
                def psb(name, shape, dt):
                    return ph.enter_context(nc.sbuf_tensor(name, shape, dt, side="right"))
                wp = [psb(f"wp{i}", [128, NCK * 128], BF16) for i in range(3)]
                xm32 = [psb(f"xm32_{i}", [128, 512], F32) for i in range(3)]
                prs = [psb(f"prs{i}", [128, 512], F32) for i in range(3)]
                res1q = [psb(f"res1q{i}", [128, NCK * 512], F32)
                         for i in range(2)]
                sqt = [psb(f"sqt{i}", [128, 512], F32) for i in range(3)]
                m_all = psb("m_all", [1, 512], F32)
                r_all = psb("r_all", [1, 512], F32)
                tmsq = psb("tmsq", [1, 512], F32)
                tm2 = psb("tm2", [1, 512], F32)
                tm3 = psb("tm3", [1, 512], F32)
                tm4 = tm2
                rtmp = tm3
                ttmp = [psb("ttmp0", [128, 512], F32)] * 2
                y32 = [psb(f"y32_{i}", [128, 512], F32) for i in range(2)]
                pp = [ph.enter_context(nc.psum_tensor(f"pp{i}", [128, 512], F32))
                      for i in range(2)]
                pst_s = ph.enter_context(nc.psum_tensor("pst_s", [1, 512], F32))
                pst_q = ph.enter_context(nc.psum_tensor("pst_q", [1, 512], F32))
                pmb = [ph.enter_context(nc.psum_tensor(f"pmb{i}", [128, 512], F32))
                       for i in range(2)]
                prb = [ph.enter_context(nc.psum_tensor(f"prb{i}", [128, 512], F32))
                       for i in range(2)]
                r_wp = DmaRing(nc, es, sems, "r_wp", n=3, k=1)
                r_x32 = DmaRing(nc, es, sems, "r_x32", n=3, k=1)
                r_y32 = DmaRing(nc, es, sems, "r_y32", n=2, k=1)
                b = snap()

                def SA_evac(t):
                    return b[s_a] + (t // 16) * 17 + (t % 16) + 1

                def VA_add(t):
                    return b[v_a] + (t // 16) * 17 + (t % 16) + 1

                with nc.Block() as blk:
                    @blk.gpsimd
                    def _(e):
                        barrier(e, b)
                        for qt in range(4):
                            for dco in range(NCK):
                                t = qt * 16 + dco
                                if t >= 3:
                                    e.wait_ge(t_a.h, b[t_a] + t - 2)
                                r_wp.issue(
                                    e, t,
                                    out=wp[t % 3][:].rearrange("p (a o) -> p a o", a=NCK),
                                    in_=wproj[:, dco * 128:(dco + 1) * 128]
                                    .rearrange("(a p) o -> p a o", p=128))

                    @blk.sync
                    def _(e):
                        barrier(e, b)
                        def ystores(nq):
                            for dco in range(NCK):
                                t = nq * 16 + dco
                                e.wait_ge(v_b.h, b[v_b] + t + 1)
                                r_y32.issue(e, t,
                                            out=yscr[dco, :, nq * 512:(nq + 1) * 512],
                                            in_=y32[dco % 2][:])
                        for qt in range(4):
                            for dco in range(NCK):
                                t = qt * 16 + dco
                                if t >= 3:
                                    e.wait_ge(v_a.h, VA_add(t - 3))
                                r_x32.issue(e, t, out=xm32[t % 3][:],
                                            in_=xm[dco, :, qt * 512:(qt + 1) * 512])
                            if qt >= 1:
                                ystores(qt - 1)
                        ystores(3)

                    @blk.tensor
                    def _(e):
                        barrier(e, b)
                        def bcast(pq):
                            e.wait_ge(s_a.h, b[s_a] + (pq + 1) * 17)
                            e.matmul(pmb[pq % 2][:], ones_fr[0:1, :],
                                     m_all[0:1, :], start=True, stop=True)
                            mm = e.matmul(prb[pq % 2][:], ones_fr[0:1, :],
                                          r_all[0:1, :], start=True, stop=True)
                            t_c.inc(mm)
                        for qt in range(4):
                            for dco in range(NCK):
                                t = qt * 16 + dco
                                r_wp.wait(e, t)
                                if t >= 2:
                                    e.wait_ge(s_a.h, SA_evac(t - 2))
                                for dc in range(NCK):
                                    mm = e.matmul(
                                        pp[t % 2][:],
                                        wp[t % 3][:, dc * 128:(dc + 1) * 128],
                                        qtsb[:, dc * TOK + qt * 512:dc * TOK + (qt + 1) * 512],
                                        start=(dc == 0), stop=(dc == 15))
                                t_a.inc(mm)
                                if dco == 0 and qt >= 1:
                                    bcast(qt - 1)
                                if dco >= 2:
                                    pdc = dco - 2
                                    pt = qt * 16 + pdc
                                    e.wait_ge(v_a.h, VA_add(pt))
                                    e.matmul(pst_s[0:1, :], ones_fc[:, 0:1],
                                             res1q[qt % 2][:, pdc * 512:(pdc + 1) * 512],
                                             start=(pdc == 0), stop=False)
                                    mm = e.matmul(pst_q[0:1, :], ones_fc[:, 0:1],
                                                  sqt[pt % 3][:],
                                                  start=(pdc == 0), stop=False)
                                    t_b.inc(mm)
                            for pdc in (14, 15):
                                pt = qt * 16 + pdc
                                e.wait_ge(v_a.h, VA_add(pt))
                                e.matmul(pst_s[0:1, :], ones_fc[:, 0:1],
                                         res1q[qt % 2][:, pdc * 512:(pdc + 1) * 512],
                                         start=False, stop=(pdc == 15))
                                mm = e.matmul(pst_q[0:1, :], ones_fc[:, 0:1],
                                              sqt[pt % 3][:],
                                              start=False, stop=(pdc == 15))
                                t_b.inc(mm)
                        bcast(3)

                    @blk.scalar
                    def _(e):
                        barrier(e, b)
                        for qt in range(4):
                            for dco in range(NCK):
                                t = qt * 16 + dco
                                e.wait_ge(t_a.h, b[t_a] + t + 1)
                                if t >= 3:
                                    e.wait_ge(v_a.h, VA_add(t - 3))
                                s_a.inc(e.activation(prs[t % 3][:], pp[t % 2][:],
                                                     AF.Identity,
                                                     bias=bpc[:, dco:dco + 1]))
                            e.wait_ge(v_a.h, b[v_a] + qt * 17 + 17)
                            s_a.inc(e.activation(
                                r_all[0:1, :],
                                rtmp[0:1, :], AF.Sqrt))

                    @blk.vector
                    def _(e):
                        barrier(e, b)
                        def normalize(nq):
                            for dco in range(NCK):
                                t = nq * 16 + dco
                                if t >= 2:
                                    r_y32.wait(e, t - 2)
                                vsync(e, e.tensor_tensor(
                                    ttmp[dco % 2][:],
                                    res1q[nq % 2][:, dco * 512:(dco + 1) * 512],
                                    pmb[nq % 2][:], ALU.subtract))
                                v_b.inc(e.tensor_tensor(y32[dco % 2][:],
                                                        ttmp[dco % 2][:],
                                                        prb[nq % 2][:], ALU.mult))
                                e.wait_ge(v_b.h, v_b.n)
                        for qt in range(4):
                            for dco in range(NCK):
                                t = qt * 16 + dco
                                e.wait_ge(s_a.h, SA_evac(t))
                                r_x32.wait(e, t)
                                vsync(e, e.tensor_tensor(
                                    res1q[qt % 2][:, dco * 512:(dco + 1) * 512],
                                    prs[t % 3][:], xm32[t % 3][:], ALU.add))
                                v_a.inc(e.tensor_tensor(
                                    sqt[t % 3][:],
                                    res1q[qt % 2][:, dco * 512:(dco + 1) * 512],
                                    res1q[qt % 2][:, dco * 512:(dco + 1) * 512],
                                    ALU.mult))
                            e.wait_ge(t_b.h, b[t_b] + (qt + 1) * 16)
                            vsync(e, e.tensor_scalar(m_all[0:1, :],
                                                     pst_s[0:1, :], 1.0 / D,
                                                     None, ALU.mult))
                            vsync(e, e.tensor_scalar(tmsq[0:1, :], pst_q[0:1, :],
                                                     1.0 / D, None, ALU.mult))
                            vsync(e, e.tensor_tensor(
                                tm2[0:1, :],
                                m_all[0:1, :],
                                m_all[0:1, :], ALU.mult))
                            vsync(e, e.tensor_tensor(tm3[0:1, :], tmsq[0:1, :],
                                                     tm2[0:1, :], ALU.subtract))
                            vsync(e, e.tensor_scalar(tm4[0:1, :], tm3[0:1, :],
                                                     EPS_LN, None, ALU.add))
                            v_a.inc(e.reciprocal(rtmp[0:1, :], tm4[0:1, :]))
                            if qt >= 1:
                                e.wait_ge(t_c.h, b[t_c] + qt)
                                normalize(qt - 1)
                        e.wait_ge(t_c.h, b[t_c] + 4)
                        normalize(3)
                t_a.n = b[t_a] + 64
                t_b.n = b[t_b] + 64
                t_c.n = b[t_c] + 4
                s_a.n = b[s_a] + 68
                v_a.n = b[v_a] + 68
                v_b.n = b[v_b] + 64

            if debug:
                b = snap()
                with nc.Block() as blk:
                    @blk.sync
                    def _(e):
                        barrier(e, b)
                        e.dma_start(out=ydump[:], in_=ysb[:]).then_inc(qs.h, 16)
                        e.wait_ge(qs.h, b[qs] + 16)
                qs.n += 16
        qt_es.close()

        if upto >= 7:
            hidden = sb("hidden", [128, JCK * 1024], BF16)
            ysbh = sb("ysbh", [128, NCK * 1024], BF16)
            r_ysb = DmaRing(nc, es, sems, "r_ysb", n=1, k=16)
            r_w12 = DmaRing(nc, es, sems, "r_w12", n=2, k=32)
            r_w3 = DmaRing(nc, es, sems, "r_w3", n=4, k=1)
            r_yl = DmaRing(nc, es, sems, "r_yl", n=2, k=1)
            r_r2s = DmaRing(nc, es, sems, "r_r2s", n=2, k=1)
            r_rl = DmaRing(nc, es, sems, "r_rl", n=2, k=1)
            r_out = DmaRing(nc, es, sems, "r_out", n=2, k=1)
            for th in range(_FF_TH):
                toff = th * 1024
                th_es = ExitStack()
                m2s = th_es.enter_context(
                    nc.sbuf_tensor(f"m2s{th}", [1, 1024], F32, side="right"))
                r2s = th_es.enter_context(
                    nc.sbuf_tensor(f"r2s{th}", [1, 1024], F32, side="right"))
                with ExitStack() as ph:
                    def psb(name, shape, dt):
                        return ph.enter_context(nc.sbuf_tensor(name, shape, dt, side="right"))
                    w1t = [psb(f"w1t{th}_{i}", [128, NCK * 512], BF16)
                           for i in range(2)]
                    w2t = [psb(f"w2t{th}_{i}", [128, NCK * 512], BF16)
                           for i in range(2)]
                    s1s = [psb(f"s1s{th}_{i}", [128, 1024], BF16)
                           for i in range(2)]
                    px1 = [ph.enter_context(
                        nc.psum_tensor(f"px1{th}_{i}", [128, 1024], F32))
                        for i in range(2)]
                    px2 = [ph.enter_context(
                        nc.psum_tensor(f"px2{th}_{i}", [128, 1024], F32))
                        for i in range(2)]
                    b = snap()
                    with nc.Block() as blk:
                        @blk.gpsimd
                        def _(e):
                            barrier(e, b)
                            for dc in range(NCK):
                                r_ysb.issue(e, th,
                                            out=ysbh[:, dc * 1024:(dc + 1) * 1024],
                                            in_=yscr[dc, :, toff:toff + 1024])

                        @blk.sync
                        def _(e):
                            barrier(e, b)
                            for jg in range(11):
                                if jg >= 2:
                                    e.wait_ge(t_a.h, b[t_a] + (jg - 1) * 4)
                                for dc in range(NCK):
                                    r_w12.issue(
                                        e, th * 11 + jg,
                                        out=w1t[jg % 2][:, dc * 512:(dc + 1) * 512],
                                        in_=w1q[jg * 512:(jg + 1) * 512,
                                                dc * 128:(dc + 1) * 128],
                                        transpose=True)
                                for dc in range(NCK):
                                    r_w12.issue(
                                        e, th * 11 + jg,
                                        out=w2t[jg % 2][:, dc * 512:(dc + 1) * 512],
                                        in_=w2q[jg * 512:(jg + 1) * 512,
                                                dc * 128:(dc + 1) * 128],
                                        transpose=True)

                        @blk.tensor
                        def _(e):
                            barrier(e, b)
                            r_ysb.wait(e, th)
                            for jg in range(11):
                                r_w12.wait(e, th * 11 + jg)
                                for jc2 in range(4):
                                    jc = jg * 4 + jc2
                                    if jc >= 2:
                                        e.wait_ge(v_a.h, b[v_a] + jc - 1)
                                    for q2 in range(2):
                                        for dc in range(NCK):
                                            e.matmul(
                                                px1[jc % 2][:, q2 * 512:(q2 + 1) * 512],
                                                w1t[jg % 2][:, dc * 512 + jc2 * 128:
                                                             dc * 512 + jc2 * 128 + 128],
                                                ysbh[:, dc * 1024 + q2 * 512:
                                                     dc * 1024 + (q2 + 1) * 512],
                                                start=(dc == 0), stop=(dc == 15))
                                    for q2 in range(2):
                                        for dc in range(NCK):
                                            mm = e.matmul(
                                                px2[jc % 2][:, q2 * 512:(q2 + 1) * 512],
                                                w2t[jg % 2][:, dc * 512 + jc2 * 128:
                                                             dc * 512 + jc2 * 128 + 128],
                                                ysbh[:, dc * 1024 + q2 * 512:
                                                     dc * 1024 + (q2 + 1) * 512],
                                                start=(dc == 0), stop=(dc == 15))
                                    t_a.inc(mm)

                        @blk.scalar
                        def _(e):
                            barrier(e, b)
                            for jc in range(JCK):
                                e.wait_ge(t_a.h, b[t_a] + jc + 1)
                                if jc >= 2:
                                    e.wait_ge(v_a.h, b[v_a] + jc - 1)
                                s_a.inc(e.activation(s1s[jc % 2][:],
                                                     px1[jc % 2][:],
                                                     AF.Sigmoid))

                        @blk.vector
                        def _(e):
                            barrier(e, b)
                            for jc in range(JCK):
                                e.wait_ge(s_a.h, b[s_a] + jc + 1)
                                v_a.inc(e.tensor_tensor(
                                    hidden[:, jc * 1024:(jc + 1) * 1024],
                                    s1s[jc % 2][:], px2[jc % 2][:], ALU.mult))
                    t_a.n = b[t_a] + JCK
                    s_a.n = b[s_a] + JCK
                    v_a.n = b[v_a] + JCK

                if _FF_PARTS < 2:
                    th_es.close()
                    continue
                with ExitStack() as ph:
                    def psb(name, shape, dt):
                        return ph.enter_context(nc.sbuf_tensor(name, shape, dt, side="right"))
                    w3t = [psb(f"w3t{th}_{i}", [128, 256], BF16) for i in range(4)]
                    y32l = [psb(f"y32l{th}_{i}", [128, 512], F32) for i in range(2)]
                    r2sb = [psb(f"r2sb{th}_{i}", [128, 512], F32) for i in range(2)]
                    r2sq = [psb(f"r2sq{th}_{i}", [128, 512], F32) for i in range(2)]
                    tms = psb(f"tms{th}", [1, 512], F32)
                    tmm = psb(f"tmm{th}", [1, 512], F32)
                    tmn = psb(f"tmn{th}", [1, 512], F32)
                    tmo = psb(f"tmo{th}", [1, 512], F32)
                    rtm2 = psb(f"rtm2{th}", [1, 1024], F32)
                    px3 = [ph.enter_context(
                        nc.psum_tensor(f"px3{th}_{i}", [128, 1024], F32))
                        for i in range(2)]
                    pls = [ph.enter_context(
                        nc.psum_tensor(f"pls{th}_{i}", [1, 512], F32))
                        for i in range(2)]
                    plq = [ph.enter_context(
                        nc.psum_tensor(f"plq{th}_{i}", [1, 512], F32))
                        for i in range(2)]
                    b = snap()

                    with nc.Block() as blk:
                        @blk.scalar
                        def _(e):
                            barrier(e, b)
                            for q2 in range(2):
                                e.wait_ge(v_a.h, b[v_a] + 32 + q2 + 1)
                                s_a.inc(e.activation(
                                    r2s[0:1, q2 * 512:(q2 + 1) * 512],
                                    rtm2[0:1, q2 * 512:(q2 + 1) * 512], AF.Sqrt))

                        @blk.sync
                        def _(e):
                            barrier(e, b)
                            for dg in range(8):
                                for hc in range(44):
                                    n = dg * 44 + hc
                                    if n >= 4:
                                        e.wait_ge(t_b.h, b[t_b] + n - 3)
                                    r_w3.issue(e, th * 352 + n,
                                               out=w3t[n % 4][:],
                                               in_=w3q[dg * 256:(dg + 1) * 256,
                                                       hc * 128:(hc + 1) * 128],
                                               transpose=True)
                                for u4 in range(4):
                                    u = dg * 4 + u4
                                    dc = dg * 2 + u4 // 2
                                    q2 = u4 % 2
                                    if u >= 2:
                                        e.wait_ge(v_a.h, b[v_a] + u - 1)
                                    r_yl.issue(
                                        e, th * 32 + u,
                                        out=y32l[u % 2][:],
                                        in_=yscr[dc, :, toff + q2 * 512:
                                                 toff + (q2 + 1) * 512])
                                for u4 in range(4):
                                    u = dg * 4 + u4
                                    dc = dg * 2 + u4 // 2
                                    q2 = u4 % 2
                                    e.wait_ge(v_a.h, b[v_a] + u + 1)
                                    r_r2s.issue(
                                        e, th * 32 + u,
                                        out=r2scr[dc, :, toff + q2 * 512:
                                                  toff + (q2 + 1) * 512],
                                        in_=r2sb[u % 2][:])

                        @blk.tensor
                        def _(e):
                            barrier(e, b)
                            for dg in range(8):
                                for hc in range(44):
                                    n = dg * 44 + hc
                                    r_w3.wait(e, th * 352 + n)
                                    if hc == 0 and dg >= 1:
                                        e.wait_ge(v_a.h, b[v_a] + dg * 4)
                                    for dcw in range(2):
                                        for q2 in range(2):
                                            mm = e.matmul(
                                                px3[dcw][:, q2 * 512:(q2 + 1) * 512],
                                                w3t[n % 4][:, dcw * 128:(dcw + 1) * 128],
                                                hidden[:, hc * 1024 + q2 * 512:
                                                       hc * 1024 + (q2 + 1) * 512],
                                                start=(hc == 0), stop=(hc == 43))
                                    t_b.inc(mm)
                                for u4 in range(4):
                                    u = dg * 4 + u4
                                    dcw = u4 // 2
                                    q2 = u4 % 2
                                    e.wait_ge(v_a.h, b[v_a] + u + 1)
                                    e.matmul(pls[q2][0:1, :], ones_fc[:, 0:1],
                                             r2sb[u % 2][:],
                                             start=(dg == 0 and dcw == 0),
                                             stop=(dg == 7 and dcw == 1))
                                    mm = e.matmul(plq[q2][0:1, :], ones_fc[:, 0:1],
                                                  r2sq[u % 2][:],
                                                  start=(dg == 0 and dcw == 0),
                                                  stop=(dg == 7 and dcw == 1))
                                    t_c.inc(mm)

                        @blk.vector
                        def _(e):
                            barrier(e, b)
                            for dg in range(8):
                                for u4 in range(4):
                                    u = dg * 4 + u4
                                    dcw = u4 // 2
                                    q2 = u4 % 2
                                    e.wait_ge(t_b.h, b[t_b] + (dg + 1) * 44)
                                    r_yl.wait(e, th * 32 + u)
                                    if u >= 2:
                                        e.wait_ge(t_c.h, b[t_c] + u - 1)
                                        r_r2s.wait(e, th * 32 + u - 2)
                                    vsync(e, e.tensor_tensor(
                                        r2sb[u % 2][:],
                                        px3[dcw][:, q2 * 512:(q2 + 1) * 512],
                                        y32l[u % 2][:], ALU.add))
                                    v_a.inc(e.tensor_tensor(r2sq[u % 2][:],
                                                            r2sb[u % 2][:],
                                                            r2sb[u % 2][:],
                                                            ALU.mult))
                            for q2 in range(2):
                                e.wait_ge(t_c.h, b[t_c] + 32)
                                vsync(e, e.tensor_scalar(
                                    m2s[0:1, q2 * 512:(q2 + 1) * 512],
                                    pls[q2][0:1, :], 1.0 / D, None, ALU.mult))
                                vsync(e, e.tensor_scalar(tms[0:1, :],
                                                         plq[q2][0:1, :],
                                                         1.0 / D, None, ALU.mult))
                                vsync(e, e.tensor_tensor(
                                    tmm[0:1, :],
                                    m2s[0:1, q2 * 512:(q2 + 1) * 512],
                                    m2s[0:1, q2 * 512:(q2 + 1) * 512], ALU.mult))
                                vsync(e, e.tensor_tensor(tmn[0:1, :], tms[0:1, :],
                                                         tmm[0:1, :], ALU.subtract))
                                vsync(e, e.tensor_scalar(tmo[0:1, :], tmn[0:1, :],
                                                         EPS_LN, None, ALU.add))
                                v_a.inc(e.reciprocal(
                                    rtm2[0:1, q2 * 512:(q2 + 1) * 512],
                                    tmo[0:1, :]))

                    t_b.n = b[t_b] + 8 * 44
                    t_c.n = b[t_c] + 32
                    v_a.n = b[v_a] + 34
                    s_a.n = b[s_a] + 2

                if _FF_PARTS < 3:
                    th_es.close()
                    continue
                with ExitStack() as ph:
                    def psb(name, shape, dt):
                        return ph.enter_context(nc.sbuf_tensor(name, shape, dt, side="right"))
                    r2l = [psb(f"r2l{th}_{i}", [128, 512], F32) for i in range(2)]
                    ot1 = [psb(f"ot1{th}_{i}", [128, 512], F32) for i in range(2)]
                    ot2 = [psb(f"ot2{th}_{i}", [128, 512], F32) for i in range(2)]
                    pmb2 = ph.enter_context(
                        nc.psum_tensor(f"pmb2{th}", [128, 512], F32))
                    prb2 = ph.enter_context(
                        nc.psum_tensor(f"prb2{th}", [128, 512], F32))
                    b = snap()

                    with nc.Block() as blk:
                        @blk.sync
                        def _(e):
                            barrier(e, b)
                            tgt = out if _NORM_TARGET_OUT else yscr
                            for u in range(32):
                                q2, dc = u // 16, u % 16
                                if u >= 2:
                                    e.wait_ge(v_b.h, b[v_b] + u - 1)
                                r_rl.issue(e, th * 32 + u,
                                           out=r2l[u % 2][:],
                                           in_=r2scr[dc, :, toff + q2 * 512:
                                                     toff + (q2 + 1) * 512])
                                if u >= 2 and _NORM_STORES:
                                    up = u - 2
                                    pq2, pdc = up // 16, up % 16
                                    r_out.issue(e, th * 32 + up,
                                                out=tgt[pdc, :, toff + pq2 * 512:
                                                        toff + (pq2 + 1) * 512],
                                                in_=ot2[up % 2][:])
                            if _NORM_STORES:
                                for up in (30, 31):
                                    e.wait_ge(v_b.h, b[v_b] + up + 1)
                                    pq2, pdc = up // 16, up % 16
                                    r_out.issue(e, th * 32 + up,
                                                out=tgt[pdc, :, toff + pq2 * 512:
                                                        toff + (pq2 + 1) * 512],
                                                in_=ot2[up % 2][:])
                            if th == 1 and _NORM_STORES:
                                e.wait_ge(r_out.c[0].h, r_out.c[0].n)
                                e.wait_ge(r_out.c[1].h, r_out.c[1].n)

                        @blk.tensor
                        def _(e):
                            barrier(e, b)
                            for q2 in range(2):
                                if not _NORM_BCAST:
                                    t_c.inc(e.nop())
                                    continue
                                if q2 >= 1:
                                    e.wait_ge(v_b.h, b[v_b] + 16)
                                e.matmul(pmb2[:], ones_fr[0:1, :],
                                         m2s[0:1, q2 * 512:(q2 + 1) * 512],
                                         start=True, stop=True)
                                mm = e.matmul(prb2[:], ones_fr[0:1, :],
                                              r2s[0:1, q2 * 512:(q2 + 1) * 512],
                                              start=True, stop=True)
                                t_c.inc(mm)

                        @blk.vector
                        def _(e):
                            barrier(e, b)
                            for q2 in range(2):
                                for dc in range(NCK):
                                    u = q2 * 16 + dc
                                    e.wait_ge(t_c.h, b[t_c] + q2 + 1)
                                    r_rl.wait(e, th * 32 + u)
                                    if u >= 2 and _NORM_STORES:
                                        r_out.wait(e, th * 32 + u - 2)
                                    if _NORM_COMPUTE:
                                        vsync(e, e.tensor_tensor(ot1[u % 2][:],
                                                                 r2l[u % 2][:],
                                                                 pmb2[:], ALU.subtract))
                                        v_b.inc(e.tensor_tensor(ot2[u % 2][:],
                                                                ot1[u % 2][:],
                                                                prb2[:], ALU.mult))
                                    else:
                                        v_b.inc(e.tensor_copy(ot2[u % 2][:],
                                                              r2l[u % 2][:]))
                    t_c.n = b[t_c] + 2
                    v_b.n = b[v_b] + 32
                th_es.close()

    return nc


def kernel(L0_attn_x, L0_attn_w_qkv, L0_attn_b_qkv, L0_attn_w_proj,
           L0_attn_b_proj, L0_ffn_w1, L0_ffn_w2, L0_ffn_w3):
    x2d = np.ascontiguousarray(
        np.asarray(L0_attn_x, dtype=np.float32).reshape(2 * LK, D))
    wqkv = np.ascontiguousarray(np.asarray(L0_attn_w_qkv, dtype=np.float32))
    bqkv = np.asarray(L0_attn_b_qkv, dtype=np.float32)
    wproj = np.ascontiguousarray(np.asarray(L0_attn_w_proj, dtype=np.float32))
    bproj = np.asarray(L0_attn_b_proj, dtype=np.float32)
    w1 = np.ascontiguousarray(np.asarray(L0_ffn_w1, dtype=np.float32))
    w2 = np.ascontiguousarray(np.asarray(L0_ffn_w2, dtype=np.float32))
    w3 = np.ascontiguousarray(np.asarray(L0_ffn_w3, dtype=np.float32))

    bq_col = np.ascontiguousarray(bqkv.reshape(48, 128).T)
    bq_row = np.ascontiguousarray(bqkv.reshape(1, 3 * D))
    bp_col = np.ascontiguousarray(bproj.reshape(16, 128).T)

    nc = build()
    in_maps = []
    for c in range(8):
        be, qslot = c // 4, c % 4
        xbat = x2d[be * LK:(be + 1) * LK]
        xbc = np.ascontiguousarray(xbat.T).reshape(NCK, 128, LK)
        xmine = xbat[qslot * TOK:(qslot + 1) * TOK]
        xmc = np.ascontiguousarray(xmine.T).reshape(NCK, 128, TOK)
        in_maps.append({
            "xb": xbc, "xm": xmc, "wqkv": wqkv, "bq_col": bq_col,
            "bq_row": bq_row, "wproj": wproj, "bp_col": bp_col,
            "w1": w1, "w2": w2, "w3": w3,
        })
    res = run_bass_kernel_spmd(nc, in_maps, core_ids=list(range(8)))
    outp = np.empty((2 * LK, D), dtype=np.float32)
    for c in range(8):
        be, qslot = c // 4, c % 4
        o = np.asarray(res.results[c]["out"]).reshape(D, TOK)
        outp[be * LK + qslot * TOK: be * LK + (qslot + 1) * TOK] = o.T
    return outp.reshape(2, LK, D)


# version 15 (final)
# speedup vs baseline: 1.0751x; 1.0464x over previous
import numpy as np
from contextlib import ExitStack

import concourse.bass as bass
import concourse.mybir as mybir
from concourse.bass_utils import run_bass_kernel_spmd

F32 = mybir.dt.float32
BF16 = mybir.dt.bfloat16
AF = mybir.ActivationFunctionType
ALU = mybir.AluOpType
AX = mybir.AxisListType.X

D = 2048
LK = 8192
TOK = 2048
H = 16
NCK = 16
HFF = 5632
JCK = 44
EPS_LN = 1e-5
EPS_Q = 1e-8

SCALE = float(np.float32(1.0) / np.float32(np.sqrt(np.float32(128.0))))
RS = float(np.sqrt(SCALE))

_KGS = 8
_DO_V = True
_DO_NORM = True
_FF_PARTS = 3
_FF_TH = 2
_NORM_STORES = True
_NORM_COMPUTE = True
_NORM_BCAST = True
_NORM_TARGET_OUT = True
_SKIP_MID = False


class C:
    def __init__(self, h):
        self.h = h
        self.n = 0

    def inc(self, instr, k=1):
        instr.then_inc(self.h, k)
        self.n += k


class DmaRing:

    def __init__(self, nc, es, sems, name, n=2, k=1):
        self.c = [C(es.enter_context(nc.semaphore(f"{name}{i}")))
                  for i in range(n)]
        for cc in self.c:
            sems.append(cc)
        self.n = n
        self.k = k

    def issue(self, e, slot, out, in_, **kw):
        cc = self.c[slot % self.n]
        e.dma_start(out=out, in_=in_, **kw).then_inc(cc.h, 16)
        cc.n += 16

    def wait(self, e, slot):
        e.wait_ge(self.c[slot % self.n].h, (slot // self.n + 1) * self.k * 16)


def build(upto=7, debug=False):
    nc = bass.Bass("TRN2", target_bir_lowering=False)
    sck = "ExternalOutput" if debug else "Internal"

    xb = nc.dram_tensor("xb", [NCK, 128, LK], F32, kind="ExternalInput").ap()
    xm = nc.dram_tensor("xm", [NCK, 128, TOK], F32, kind="ExternalInput").ap()
    wqkv = nc.dram_tensor("wqkv", [D, 3 * D], F32, kind="ExternalInput").ap()
    bq_col = nc.dram_tensor("bq_col", [128, 48], F32, kind="ExternalInput").ap()
    bq_row = nc.dram_tensor("bq_row", [1, 3 * D], F32, kind="ExternalInput").ap()
    wproj = nc.dram_tensor("wproj", [D, D], F32, kind="ExternalInput").ap()
    bp_col = nc.dram_tensor("bp_col", [128, 16], F32, kind="ExternalInput").ap()
    w1 = nc.dram_tensor("w1", [HFF, D], F32, kind="ExternalInput").ap()
    w2 = nc.dram_tensor("w2", [HFF, D], F32, kind="ExternalInput").ap()
    w3 = nc.dram_tensor("w3", [D, HFF], F32, kind="ExternalInput").ap()
    out = nc.dram_tensor("out", [NCK, 128, TOK], F32, kind="ExternalOutput").ap()

    kscr = nc.dram_tensor("kscr", [H, 128, LK], BF16, kind=sck).ap()
    vscr = nc.dram_tensor("vscr", [H, LK, 128], BF16, kind=sck).ap()
    w1q = nc.dram_tensor("w1qs", [HFF, D], BF16, kind=sck).ap()
    w2q = nc.dram_tensor("w2qs", [HFF, D], BF16, kind=sck).ap()
    w3q = nc.dram_tensor("w3qs", [D, HFF], BF16, kind=sck).ap()
    yscr = nc.dram_tensor("yscr", [NCK, 128, TOK], F32, kind=sck).ap()
    r2scr = nc.dram_tensor("r2scr", [NCK, 128, TOK], F32, kind=sck).ap()
    if debug:
        qdump = nc.dram_tensor("qdump", [128, H * TOK], BF16,
                               kind="ExternalOutput").ap()
        ndump = nc.dram_tensor("ndump", [128, 32], F32,
                               kind="ExternalOutput").ap()
        n2dump = nc.dram_tensor("n2dump", [1, 1024], F32,
                                kind="ExternalOutput").ap()
        ydump = nc.dram_tensor("ydump", [128, NCK * TOK], BF16,
                               kind="ExternalOutput").ap()

    es = ExitStack()
    with es:
        def sb(name, shape, dt):
            return es.enter_context(nc.sbuf_tensor(name, shape, dt, side="left"))

        ones_bf = sb("ones_bf", [128, 1], BF16)
        ones_fc = sb("ones_fc", [128, 1], F32)
        ones_fr = sb("ones_fr", [64, 128], F32)
        bqc = sb("bqc", [128, 48], F32)
        bpc = sb("bpc", [128, 16], F32)
        bqrs = sb("bqrs", [128, 16], F32)
        bkrs = sb("bkrs", [128, 16], F32)
        bvb_es = ExitStack()
        bvb = bvb_es.enter_context(nc.sbuf_tensor("bvb", [128, D], F32,
                                                  side="right"))
        knmax = sb("knmax", [1, 256], F32)
        qnmax = sb("qnmax", [1, 64], F32)
        negc = sb("negc", [128, 16], F32)

        qg = C(es.enter_context(nc.semaphore("qg")))
        qs = C(es.enter_context(nc.semaphore("qs")))
        qa = C(es.enter_context(nc.semaphore("qa")))
        t_a = C(es.enter_context(nc.semaphore("t_a")))
        t_b = C(es.enter_context(nc.semaphore("t_b")))
        t_c = C(es.enter_context(nc.semaphore("t_c")))
        s_a = C(es.enter_context(nc.semaphore("s_a")))
        v_a = C(es.enter_context(nc.semaphore("v_a")))
        v_b = C(es.enter_context(nc.semaphore("v_b")))
        g_a = C(es.enter_context(nc.semaphore("g_a")))
        v_s = C(es.enter_context(nc.semaphore("v_s")))
        v_q = C(es.enter_context(nc.semaphore("v_q")))
        v_q2 = C(es.enter_context(nc.semaphore("v_q2")))
        t_q = C(es.enter_context(nc.semaphore("t_q")))
        v_r = C(es.enter_context(nc.semaphore("v_r")))
        sems = [qg, qs, qa, t_a, t_b, t_c, s_a, v_a, v_b, g_a, v_q, v_q2, t_q,
                v_r]

        def vsync(e, instr):
            v_s.inc(instr)
            e.wait_ge(v_s.h, v_s.n)

        def snap():
            return {s: s.n for s in sems}

        def barrier(e, b):
            for s, n in b.items():
                if n:
                    e.wait_ge(s.h, n)

        b = snap()
        with nc.Block() as blk:
            @blk.sync
            def _(e):
                e.dma_start(out=bqc[:], in_=bq_col[:]).then_inc(qs.h, 16)
                e.dma_start(out=bpc[:], in_=bp_col[:]).then_inc(qs.h, 16)
                e.dma_start(out=bvb[0:1, :], in_=bq_row[0:1, 4096:6144]) \
                    .then_inc(qs.h, 16)

            @blk.gpsimd
            def _(e):
                e.memset(ones_bf[:], 1.0)
                e.memset(ones_fc[:], 1.0)
                e.memset(ones_fr[:], 1.0)
                g_a.inc(e.nop())

            @blk.vector
            def _(e):
                e.wait_ge(qs.h, b[qs] + 48)
                e.tensor_scalar(bqrs[:], bqc[:, 0:16], RS, None, ALU.mult)
                v_a.inc(e.tensor_scalar(bkrs[:], bqc[:, 16:32], RS, None,
                                        ALU.mult))
        qs.n += 48

        with nc.Block() as blk, \
                nc.psum_tensor("pstp", [128, 512], F32) as pstp:
            @blk.tensor
            def _(e):
                e.wait_ge(qs.h, b[qs] + 48)
                e.wait_ge(g_a.h, b[g_a] + 1)
                for i in range(4):
                    if i > 0:
                        e.wait_ge(v_a.h, b[v_a] + 1 + i)
                    t_a.inc(e.matmul(pstp[:, 0:512], ones_fr[0:1, :],
                                     bvb[0:1, i * 512:(i + 1) * 512],
                                     start=True, stop=True))

            @blk.vector
            def _(e):
                for i in range(4):
                    e.wait_ge(t_a.h, b[t_a] + i + 1)
                    v_a.inc(e.tensor_copy(bvb[:, i * 512:(i + 1) * 512],
                                          pstp[:, 0:512]))

        QCH = []
        for wi_, (ws_, wd_, nrb_, W_) in enumerate(
                ((w1, w1q, 44, 2048), (w2, w2q, 44, 2048),
                 (w3, w3q, 16, 5632))):
            for rb_ in range(nrb_):
                for co_ in range(0, W_, 2048):
                    QCH.append((ws_, wd_, rb_, co_, min(2048, W_ - co_), wi_))
        NQ = len(QCH)
        racc = sb("racc", [128, NQ], F32)
        gi3 = sb("gi3", [128, 3], F32)
        xm16_es = ExitStack()
        xm16 = xm16_es.enter_context(nc.sbuf_tensor("xm16", [128, NCK * TOK],
                                                    BF16, side="left"))
        r_xm = DmaRing(nc, es, sems, "r_xm", n=1, k=16)
        if upto >= 2 and not _SKIP_MID:
            with ExitStack() as ph:
                def psb(name, shape, dt):
                    return ph.enter_context(nc.sbuf_tensor(name, shape, dt, side="right"))
                wkv = psb("wkv", [128, NCK * 2048], BF16)
                xg = [psb(f"xg{i}", [128, NCK * 512], BF16) for i in range(2)]
                ksb = [psb(f"ksb{i}", [128, 512], BF16) for i in range(2)]
                sqk = [psb(f"sqk{i}", [128, 512], BF16) for i in range(2)]
                vsb = [psb(f"vsb{i}", [128, 512], BF16) for i in range(2)]
                pk = [ph.enter_context(nc.psum_tensor(f"pk{i}", [128, 512], F32))
                      for i in range(2)]
                pv = [ph.enter_context(nc.psum_tensor(f"pv{i}", [128, 512], F32))
                      for i in range(2)]
                pn = [ph.enter_context(nc.psum_tensor(f"pn{i}", [1, 512], F32))
                      for i in range(2)]
                r_wkv = DmaRing(nc, es, sems, "r_wkv", n=1, k=32)
                r_xg = DmaRing(nc, es, sems, "r_xg", n=2, k=16)
                r_kst = DmaRing(nc, es, sems, "r_kst", n=2, k=1)
                r_vst = DmaRing(nc, es, sems, "r_vst", n=2, k=4)
                b = snap()

                with nc.Block() as blk:
                    @blk.gpsimd
                    def _(e):
                        barrier(e, b)
                        for hp in range(2):
                            if hp == 1:
                                e.wait_ge(t_a.h, b[t_a] + 128)
                                e.wait_ge(t_c.h, b[t_c] + 128)
                            for dc in range(NCK):
                                r_wkv.issue(e, hp,
                                            out=wkv[:, dc * 2048:dc * 2048 + 1024],
                                            in_=wqkv[dc * 128:(dc + 1) * 128,
                                                     2048 + hp * 1024:3072 + hp * 1024])
                                r_wkv.issue(e, hp,
                                            out=wkv[:, dc * 2048 + 1024:(dc + 1) * 2048],
                                            in_=wqkv[dc * 128:(dc + 1) * 128,
                                                     4096 + hp * 1024:5120 + hp * 1024])
                            for kg in range(16):
                                s = hp * 16 + kg
                                if s >= 2:
                                    e.wait_ge(t_a.h, b[t_a] + (s - 1) * 8)
                                    e.wait_ge(t_c.h, b[t_c] + (s - 1) * 8)
                                for dc in range(NCK):
                                    r_xg.issue(e, s,
                                               out=xg[s % 2][:, dc * 512:(dc + 1) * 512],
                                               in_=xb[dc, :, kg * 512:(kg + 1) * 512])
                        for dc in range(NCK):
                            r_xm.issue(e, 0, out=xm16[:, dc * TOK:(dc + 1) * TOK],
                                       in_=xm[dc, :, :])
                    @blk.tensor
                    def _(e):
                        barrier(e, b)
                        for hp in range(2):
                            r_wkv.wait(e, hp)
                            for kg in range(16):
                                s = hp * 16 + kg
                                r_xg.wait(e, s)
                                for i in range(8):
                                    idx = s * 8 + i
                                    h_l = i
                                    if idx >= 2:
                                        e.wait_ge(s_a.h, b[s_a] + idx - 1)
                                    for dc in range(NCK):
                                        mm = e.matmul(
                                            pk[idx % 2][:],
                                            wkv[:, dc * 2048 + h_l * 128:dc * 2048 + (h_l + 1) * 128],
                                            xg[s % 2][:, dc * 512:(dc + 1) * 512],
                                            start=(dc == 0), stop=(dc == 15))
                                    t_a.inc(mm)
                                    if i >= 1:
                                        p = idx - 1
                                        e.wait_ge(s_a.h, b[s_a] + p + 1)
                                        if p >= 2:
                                            e.wait_ge(v_a.h, b[v_a] + p - 1)
                                        t_b.inc(e.matmul(pn[p % 2][0:1, :],
                                                         ones_bf[:, 0:1],
                                                         sqk[p % 2][:],
                                                         start=True, stop=True))
                                p = s * 8 + 7
                                e.wait_ge(s_a.h, b[s_a] + p + 1)
                                if p >= 2:
                                    e.wait_ge(v_a.h, b[v_a] + p - 1)
                                t_b.inc(e.matmul(pn[p % 2][0:1, :],
                                                 ones_bf[:, 0:1], sqk[p % 2][:],
                                                 start=True, stop=True))
                                for j in range(8):
                                    jdx = s * 8 + j
                                    g4l, kc = j // 4, j % 4
                                    if jdx >= 2:
                                        e.wait_ge(v_b.h, b[v_b] + jdx - 1)
                                    for dc in range(NCK):
                                        mm = e.matmul(
                                            pv[jdx % 2][:],
                                            xg[s % 2][:, dc * 512 + kc * 128:dc * 512 + (kc + 1) * 128],
                                            wkv[:, dc * 2048 + 1024 + g4l * 512:dc * 2048 + 1024 + (g4l + 1) * 512],
                                            start=(dc == 0), stop=(dc == 15))
                                    t_c.inc(mm)

                    @blk.scalar
                    def _(e):
                        barrier(e, b)
                        for idx in range(256):
                            s = idx // 8
                            i = idx % 8
                            h = (s // 16) * 8 + i
                            e.wait_ge(t_a.h, b[t_a] + idx + 1)
                            if idx >= 2:
                                r_kst.wait(e, idx - 2)
                                e.wait_ge(t_b.h, b[t_b] + idx - 1)
                            e.activation(ksb[idx % 2][:], pk[idx % 2][:],
                                         AF.Identity, bias=bkrs[:, h:h + 1],
                                         scale=RS)
                            s_a.inc(e.activation(sqk[idx % 2][:],
                                                 pk[idx % 2][:], AF.Square,
                                                 bias=bkrs[:, h:h + 1],
                                                 scale=RS))

                    @blk.vector
                    def _(e):
                        barrier(e, b)
                        for hp in range(2):
                            for kg in range(16):
                                s = hp * 16 + kg
                                for i in range(8):
                                    idx = s * 8 + i
                                    h = hp * 8 + i
                                    col = h * 16 + kg
                                    e.wait_ge(t_b.h, b[t_b] + idx + 1)
                                    rm = e.reduce_max(knmax[0:1, col:col + 1],
                                                      pn[idx % 2][0:1, :],
                                                      axis=AX)
                                    vsync(e, rm)
                                    v_a.inc(e.nop())
                                for j in range(8):
                                    jdx = s * 8 + j
                                    g4l = j // 4
                                    e.wait_ge(t_c.h, b[t_c] + jdx + 1)
                                    if jdx >= 2:
                                        r_vst.wait(e, jdx - 2)
                                    v_b.inc(e.tensor_tensor(
                                        vsb[jdx % 2][:], pv[jdx % 2][:],
                                        bvb[:, (hp * 2 + g4l) * 512:(hp * 2 + g4l + 1) * 512],
                                        ALU.add))

                    @blk.sync
                    def _(e):
                        barrier(e, b)
                        for hp in range(2):
                            for kg in range(16):
                                s = hp * 16 + kg
                                for i in range(8):
                                    idx = s * 8 + i
                                    h = hp * 8 + i
                                    e.wait_ge(s_a.h, b[s_a] + idx + 1)
                                    r_kst.issue(
                                        e, idx,
                                        out=kscr[h, :, kg * 512:(kg + 1) * 512],
                                        in_=ksb[idx % 2][:])
                                for j in range(8):
                                    jdx = s * 8 + j
                                    g4l, kc = j // 4, j % 4
                                    g4 = hp * 2 + g4l
                                    e.wait_ge(v_b.h, b[v_b] + jdx + 1)
                                    for hh in range(4):
                                        r_vst.issue(
                                            e, jdx,
                                            out=vscr[g4 * 4 + hh,
                                                     kg * 512 + kc * 128:kg * 512 + (kc + 1) * 128, :],
                                            in_=vsb[jdx % 2][:, hh * 128:(hh + 1) * 128])
                t_a.n = b[t_a] + 256
                t_b.n = b[t_b] + 256
                t_c.n = b[t_c] + 256
                s_a.n = b[s_a] + 256
                v_a.n = b[v_a] + 256
                v_b.n = b[v_b] + 256


        bvb_es.close()

        qt_es = ExitStack()
        qtsb = qt_es.enter_context(nc.sbuf_tensor("qtsb", [128, H * TOK], BF16,
                                                  side="right"))
        if upto >= 3 and not _SKIP_MID:
            with ExitStack() as ph:
                def psb(name, shape, dt):
                    return ph.enter_context(nc.sbuf_tensor(name, shape, dt, side="right"))
                wqh = [psb(f"wqh{i}", [128, NCK * 128], BF16) for i in range(2)]
                sqq = [psb(f"sqq{i}", [128, 512], BF16) for i in range(2)]
                pq = [ph.enter_context(nc.psum_tensor(f"pq{i}", [128, 512], F32))
                      for i in range(2)]
                pnq = [ph.enter_context(nc.psum_tensor(f"pnq{i}", [1, 512], F32))
                       for i in range(2)]
                r_wq = DmaRing(nc, es, sems, "r_wq", n=2, k=1)
                b = snap()

                with nc.Block() as blk:
                    @blk.gpsimd
                    def _(e):
                        barrier(e, b)
                        for h in range(H):
                            if h >= 2:
                                e.wait_ge(t_a.h, b[t_a] + (h - 1) * 4)
                            r_wq.issue(
                                e, h,
                                out=wqh[h % 2][:].rearrange("p (a j) -> p a j", a=NCK),
                                in_=wqkv[:, h * 128:(h + 1) * 128]
                                .rearrange("(a p) j -> p a j", p=128))

                    @blk.tensor
                    def _(e):
                        barrier(e, b)
                        r_xm.wait(e, 0)
                        for h in range(H):
                            r_wq.wait(e, h)
                            for qt in range(4):
                                n = h * 4 + qt
                                if n >= 2:
                                    e.wait_ge(s_a.h, b[s_a] + n - 1)
                                for dc in range(NCK):
                                    mm = e.matmul(
                                        pq[n % 2][:],
                                        wqh[h % 2][:, dc * 128:(dc + 1) * 128],
                                        xm16[:, dc * TOK + qt * 512:dc * TOK + (qt + 1) * 512],
                                        start=(dc == 0), stop=(dc == 15))
                                t_a.inc(mm)
                                if n >= 1:
                                    p = n - 1
                                    e.wait_ge(s_a.h, b[s_a] + p + 1)
                                    if p >= 2:
                                        e.wait_ge(v_a.h, b[v_a] + p - 1)
                                    t_b.inc(e.matmul(pnq[p % 2][0:1, :],
                                                     ones_bf[:, 0:1],
                                                     sqq[p % 2][:],
                                                     start=True, stop=True))
                        e.wait_ge(s_a.h, b[s_a] + 64)
                        e.wait_ge(v_a.h, b[v_a] + 62)
                        t_b.inc(e.matmul(pnq[63 % 2][0:1, :], ones_bf[:, 0:1],
                                         sqq[63 % 2][:], start=True, stop=True))

                    @blk.scalar
                    def _(e):
                        barrier(e, b)
                        for n in range(64):
                            h, qt = n // 4, n % 4
                            e.wait_ge(t_a.h, b[t_a] + n + 1)
                            if n >= 2:
                                e.wait_ge(t_b.h, b[t_b] + n - 1)
                            e.activation(
                                qtsb[:, h * TOK + qt * 512:h * TOK + (qt + 1) * 512],
                                pq[n % 2][:], AF.Identity,
                                bias=bqrs[:, h:h + 1], scale=RS)
                            s_a.inc(e.activation(
                                sqq[n % 2][:], pq[n % 2][:], AF.Square,
                                bias=bqrs[:, h:h + 1], scale=RS))

                    @blk.vector
                    def _(e):
                        barrier(e, b)
                        for n in range(64):
                            e.wait_ge(t_b.h, b[t_b] + n + 1)
                            rm = e.reduce_max(qnmax[0:1, n:n + 1],
                                              pnq[n % 2][0:1, :], axis=AX)
                            vsync(e, rm)
                            v_a.inc(e.nop())
                t_a.n = b[t_a] + 64
                t_b.n = b[t_b] + 64
                s_a.n = b[s_a] + 64
                v_a.n = b[v_a] + 64

            with ExitStack() as ph, \
                    nc.psum_tensor("pnc", [128, 16], F32) as pnc:
                km2 = ph.enter_context(nc.sbuf_tensor("km2", [1, 16], F32, side="right"))
                qm2 = ph.enter_context(nc.sbuf_tensor("qm2", [1, 16], F32, side="right"))
                ncr = ph.enter_context(nc.sbuf_tensor("ncr", [1, 16], F32, side="right"))
                ncs = ph.enter_context(nc.sbuf_tensor("ncs", [1, 16], F32, side="right"))
                b = snap()
                with nc.Block() as blk:
                    @blk.vector
                    def _(e):
                        barrier(e, b)
                        for h in range(H):
                            vsync(e, e.reduce_max(km2[0:1, h:h + 1],
                                                  knmax[0:1, h * 16:(h + 1) * 16],
                                                  axis=AX))
                            vsync(e, e.reduce_max(qm2[0:1, h:h + 1],
                                                  qnmax[0:1, h * 4:(h + 1) * 4],
                                                  axis=AX))
                        vsync(e, e.tensor_tensor(ncr[0:1, :], km2[0:1, :],
                                                 qm2[0:1, :], ALU.add))
                        v_a.inc(e.tensor_scalar(ncs[0:1, :], ncr[0:1, :], -0.5,
                                                None, ALU.mult))
                        e.wait_ge(t_a.h, b[t_a] + 1)
                        v_a.inc(e.tensor_copy(negc[:], pnc[:, 0:16]))

                    @blk.tensor
                    def _(e):
                        e.wait_ge(v_a.h, b[v_a] + 1)
                        t_a.inc(e.matmul(pnc[:, 0:16], ones_fr[0:1, :],
                                         ncs[0:1, 0:16], start=True, stop=True))

                    if debug:
                        @blk.sync
                        def _(e):
                            e.wait_ge(v_a.h, b[v_a] + 2)
                            e.dma_start(out=n2dump[0:1, 320:336], in_=km2[0:1, :]) \
                                .then_inc(qs.h, 16)
                            e.dma_start(out=n2dump[0:1, 336:352], in_=qm2[0:1, :]) \
                                .then_inc(qs.h, 16)
                            e.dma_start(out=n2dump[0:1, 352:368], in_=ncs[0:1, :]) \
                                .then_inc(qs.h, 16)
                            e.dma_start(out=n2dump[0:1, 368:384], in_=ncr[0:1, :]) \
                                .then_inc(qs.h, 16)
                        qs.n += 64

            if debug and upto == 3:
                b = snap()
                with nc.Block() as blk:
                    @blk.sync
                    def _(e):
                        barrier(e, b)
                        e.dma_start(out=qdump[:], in_=qtsb[:]).then_inc(qs.h, 16)
                        e.dma_start(out=ndump[:, 0:16], in_=negc[:]) \
                            .then_inc(qs.h, 16)
                        e.dma_start(out=n2dump[0:1, 0:256], in_=knmax[0:1, :]) \
                            .then_inc(qs.h, 16)
                        e.dma_start(out=n2dump[0:1, 256:320], in_=qnmax[0:1, :]) \
                            .then_inc(qs.h, 16)
                        e.wait_ge(qs.h, b[qs] + 64)
                qs.n += 64


        xm16_es.close()

        if upto >= 4 and not _SKIP_MID:
            with ExitStack() as ph:
                def psb(name, shape, dt):
                    return ph.enter_context(nc.sbuf_tensor(name, shape, dt, side="right"))
                kh = [psb(f"kh{i}", [128, LK], BF16) for i in range(2)]
                vh = [psb(f"vh{i}", [128, LK], BF16) for i in range(2)]
                exs = [psb(f"exs{i}", [128, 1024], BF16) for i in range(4)]
                rlsb = psb("rlsb", [64, 512], F32)
                rbsb = psb("rbsb", [128, 1024], F32)
                ps_s = [ph.enter_context(nc.psum_tensor(f"ps_s{i}", [128, 1024], F32))
                        for i in range(2)]
                po = [ph.enter_context(nc.psum_tensor(f"po{i}", [128, 512], F32))
                      for i in range(2)]
                pl = [ph.enter_context(nc.psum_tensor(f"pl{i}", [128, 512], F32))
                      for i in range(2)]
                wbuf2 = [psb(f"wbuf2_{i}", [128, 2048], F32) for i in range(3)]
                tq1 = psb("tq1", [128, 2048], BF16)
                tq2 = psb("tq2", [128, 2048], BF16)
                wqb2 = [psb(f"wqb2_{i}", [128, 2048], BF16) for i in range(2)]
                rtot3 = psb("rtot3", [128, 3], F32)
                gsc3 = psb("gsc3", [1, 3], F32)
                grc3 = psb("grc3", [1, 3], F32)
                r_kv = DmaRing(nc, es, sems, "r_kv", n=2, k=2)
                r_q1 = DmaRing(nc, es, sems, "r_q1", n=3, k=1)
                r_q2 = DmaRing(nc, es, sems, "r_q2", n=3, k=1)
                r_q2s = DmaRing(nc, es, sems, "r_q2s", n=2, k=1)
                NW = 44 * 128 * 2048
                _q1c = [23, 23, 23, 23, 22, 22] + [0] * 10
                q1_of_h = []
                _qa = 0
                for h_ in range(H):
                    q1_of_h.append(list(range(_qa, _qa + _q1c[h_])))
                    _qa += _q1c[h_]
                q2_of_h = [list(range((h_ - 7) * 17, (h_ - 6) * 17))
                           if 7 <= h_ <= 14 else [] for h_ in range(H)]
                b = snap()

                with nc.Block() as blk:
                    @blk.sync
                    def _(e):
                        barrier(e, b)
                        for h in range(H):
                            if h >= 2:
                                e.wait_ge(t_a.h, b[t_a] + (h - 1) * 128)
                                e.wait_ge(t_b.h, b[t_b] + (h - 1) * 128)
                            r_kv.issue(e, h, out=kh[h % 2][:], in_=kscr[h, :, :])
                            r_kv.issue(
                                e, h,
                                out=vh[h % 2][:].rearrange("p (a d) -> p a d", a=64),
                                in_=vscr[h].rearrange("(a p) d -> p a d", p=128))

                    @blk.gpsimd
                    def _(e):
                        barrier(e, b)
                        for ci in range(NQ):
                            ws_, wd_, rb_, co_, wdt_, wi_ = QCH[ci]
                            if ci >= 3:
                                e.wait_ge(v_q.h, b[v_q] + ci - 2)
                            r_q1.issue(e, ci, out=wbuf2[ci % 3][:, 0:wdt_],
                                       in_=ws_[rb_ * 128:(rb_ + 1) * 128,
                                               co_:co_ + wdt_])
                        e.wait_ge(v_q.h, b[v_q] + NQ)
                        for ci in range(NQ):
                            ws_, wd_, rb_, co_, wdt_, wi_ = QCH[ci]
                            if ci >= 3:
                                e.wait_ge(v_q2.h, b[v_q2] + ci - 2)
                            r_q2.issue(e, ci, out=wbuf2[ci % 3][:, 0:wdt_],
                                       in_=ws_[rb_ * 128:(rb_ + 1) * 128,
                                               co_:co_ + wdt_])
                            if ci >= 2:
                                up = ci - 2
                                us_, ud_, urb, uco, uwd, uwi = QCH[up]
                                e.wait_ge(v_q2.h, b[v_q2] + up + 1)
                                r_q2s.issue(e, up,
                                            out=ud_[urb * 128:(urb + 1) * 128,
                                                    uco:uco + uwd],
                                            in_=wqb2[up % 2][:, 0:uwd])
                        e.wait_ge(v_q2.h, b[v_q2] + NQ)
                        for up in (NQ - 2, NQ - 1):
                            us_, ud_, urb, uco, uwd, uwi = QCH[up]
                            r_q2s.issue(e, up,
                                        out=ud_[urb * 128:(urb + 1) * 128,
                                                uco:uco + uwd],
                                        in_=wqb2[up % 2][:, 0:uwd])

                    @blk.tensor
                    def _(e):
                        barrier(e, b)
                        for h in range(H):
                            if h == 7:
                                e.wait_ge(v_q.h, b[v_q] + NQ + 3)
                            for qp in range(2):
                                g = h * 2 + qp
                                eb = b[s_a] + g * 64
                                if qp == 0:
                                    r_kv.wait(e, h)
                                for kc in range(64):
                                    for q2 in range(2):
                                        mm = e.matmul(
                                            ps_s[kc % 2][:, q2 * 512:(q2 + 1) * 512],
                                            kh[h % 2][:, kc * 128:(kc + 1) * 128],
                                            qtsb[:, h * TOK + qp * 1024 + q2 * 512:
                                                 h * TOK + qp * 1024 + (q2 + 1) * 512],
                                            start=True, stop=True)
                                    t_a.inc(mm)
                                    if kc == 1 and g >= 1:
                                        if g != 14:
                                            e.wait_ge(v_r.h, b[v_r] + g)
                                            e.matmul(
                                                pl[1][:], ones_fr[0:1, :],
                                                rlsb[0:1, :], start=True,
                                                stop=True,
                                                tile_position=(0, 0))
                                            e.wait_ge(v_a.h, b[v_a] + g)
                                            mm = e.matmul(
                                                pl[0][:], ones_fr[32:33, :],
                                                rlsb[32:33, :], start=True,
                                                stop=True,
                                                tile_position=(32, 0))
                                            t_c.inc(mm)
                                        e.wait_ge(v_b.h, b[v_b] + g)
                                    if kc >= 1:
                                        p = kc - 1
                                        e.wait_ge(s_a.h, eb + p + 1)
                                        for q2 in range(2):
                                            e.matmul(po[q2][:],
                                                     vh[h % 2][:, p * 128:(p + 1) * 128],
                                                     exs[p % 4][:, q2 * 512:(q2 + 1) * 512],
                                                     start=(p == 0), stop=False)
                                        for q2 in range(2):
                                            mm = e.matmul(pl[0][q2 * 32:q2 * 32 + 1, :],
                                                          ones_bf[:, 0:1],
                                                          exs[p % 4][:, q2 * 512:(q2 + 1) * 512],
                                                          start=(p == 0), stop=False,
                                                          tile_position=(0, q2 * 32))
                                        t_b.inc(mm)
                                e.wait_ge(s_a.h, eb + 64)
                                for q2 in range(2):
                                    e.matmul(po[q2][:], vh[h % 2][:, 63 * 128:64 * 128],
                                             exs[63 % 4][:, q2 * 512:(q2 + 1) * 512],
                                             start=False, stop=True)
                                for q2 in range(2):
                                    mm = e.matmul(pl[0][q2 * 32:q2 * 32 + 1, :],
                                                  ones_bf[:, 0:1],
                                                  exs[63 % 4][:, q2 * 512:(q2 + 1) * 512],
                                                  start=False, stop=True,
                                                  tile_position=(0, q2 * 32))
                                t_b.inc(mm)
                            if h == 6:
                                e.wait_ge(v_r.h, b[v_r] + 14)
                                e.matmul(pl[1][:], ones_fr[0:1, :],
                                         rlsb[0:1, :], start=True, stop=True,
                                         tile_position=(0, 0))
                                e.wait_ge(v_a.h, b[v_a] + 14)
                                mm = e.matmul(pl[0][:], ones_fr[32:33, :],
                                              rlsb[32:33, :], start=True,
                                              stop=True,
                                              tile_position=(32, 0))
                                t_c.inc(mm)
                                e.wait_ge(v_q.h, b[v_q] + NQ + 1)
                                t_q.inc(e.matmul(ps_s[0][0:1, 0:3],
                                                 ones_fc[:, 0:1], rtot3[:, 0:3],
                                                 start=True, stop=True))
                                e.wait_ge(v_q.h, b[v_q] + NQ + 2)
                                t_q.inc(e.matmul(ps_s[1][:, 0:3], ones_fr[0:1, :],
                                                 grc3[0:1, 0:3],
                                                 start=True, stop=True))
                        e.wait_ge(v_r.h, b[v_r] + 32)
                        e.matmul(pl[1][:], ones_fr[0:1, :], rlsb[0:1, :],
                                 start=True, stop=True, tile_position=(0, 0))
                        e.wait_ge(v_a.h, b[v_a] + 32)
                        mm = e.matmul(pl[0][:], ones_fr[32:33, :],
                                      rlsb[32:33, :], start=True, stop=True,
                                      tile_position=(32, 0))
                        t_c.inc(mm)

                    @blk.scalar
                    def _(e):
                        barrier(e, b)
                        for h in range(H):
                            for qp in range(2):
                                g = h * 2 + qp
                                for kc in range(64):
                                    e.wait_ge(t_a.h, b[t_a] + g * 64 + kc + 1)
                                    if kc >= 4:
                                        e.wait_ge(t_b.h, b[t_b] + g * 64 + kc - 3)
                                    s_a.inc(e.activation(exs[kc % 4][:],
                                                         ps_s[kc % 2][:], AF.Exp,
                                                         bias=negc[:, h:h + 1]))

                    @blk.vector
                    def _(e):
                        barrier(e, b)
                        for h in range(H):
                            for qp in range(2):
                                g = h * 2 + qp
                                e.wait_ge(t_b.h, b[t_b] + (g + 1) * 64)
                                v_r.inc(e.reciprocal(rlsb[0:1, :],
                                                     pl[0][0:1, :]))
                                v_a.inc(e.reciprocal(rlsb[32:33, :],
                                                     pl[0][32:33, :]))
                                e.wait_ge(t_c.h, b[t_c] + g + 1)
                                e.tensor_copy(rbsb[:, 0:512], pl[1][:])
                                vsync(e, e.tensor_copy(rbsb[:, 512:1024],
                                                       pl[0][:]))
                                for q2 in range(2):
                                    col = h * TOK + qp * 1024 + q2 * 512
                                    mm = e.tensor_tensor(
                                        qtsb[:, col:col + 512], po[q2][:],
                                        rbsb[:, q2 * 512:(q2 + 1) * 512],
                                        ALU.mult)
                                v_b.inc(mm)
                                e.wait_ge(v_b.h, v_b.n)
                            if qp == 1:
                                for ci in q1_of_h[h]:
                                    wdt_ = QCH[ci][4]
                                    r_q1.wait(e, ci)
                                    rs = e.reduce_sum(racc[:, ci:ci + 1],
                                                      wbuf2[ci % 3][:, 0:wdt_],
                                                      axis=AX,
                                                      apply_absolute_value=True)
                                    vsync(e, rs)
                                    v_q.inc(e.nop())
                                if h == 6:
                                    vsync(e, e.reduce_sum(rtot3[:, 0:1],
                                                          racc[:, 0:44], axis=AX))
                                    vsync(e, e.reduce_sum(rtot3[:, 1:2],
                                                          racc[:, 44:88], axis=AX))
                                    vsync(e, e.reduce_sum(rtot3[:, 2:3],
                                                          racc[:, 88:NQ], axis=AX))
                                    v_q.inc(e.nop())
                                    e.wait_ge(t_q.h, b[t_q] + 1)
                                    vsync(e, e.tensor_scalar(
                                        gsc3[0:1, 0:3], ps_s[0][0:1, 0:3],
                                        1.0 / NW, EPS_Q, ALU.mult, ALU.add))
                                    vsync(e, e.reciprocal(grc3[0:1, 0:3],
                                                          gsc3[0:1, 0:3]))
                                    v_q.inc(e.nop())
                                    e.wait_ge(t_q.h, b[t_q] + 2)
                                    vsync(e, e.tensor_copy(gi3[:],
                                                           ps_s[1][:, 0:3]))
                                    v_q.inc(e.nop())
                                for ci in q2_of_h[h]:
                                    ws_, wd_, rb_, co_, wdt_, wi_ = QCH[ci]
                                    r_q2.wait(e, ci)
                                    if ci >= 2:
                                        r_q2s.wait(e, ci - 2)
                                    vsync(e, e.tensor_scalar(
                                        tq1[:, 0:wdt_], wbuf2[ci % 3][:, 0:wdt_],
                                        gi3[:, wi_:wi_ + 1], 0.5, ALU.mult,
                                        ALU.is_gt))
                                    vsync(e, e.tensor_scalar(
                                        tq2[:, 0:wdt_], wbuf2[ci % 3][:, 0:wdt_],
                                        gi3[:, wi_:wi_ + 1], -0.5, ALU.mult,
                                        ALU.is_lt))
                                    v_q2.inc(e.tensor_tensor(
                                        wqb2[ci % 2][:, 0:wdt_], tq1[:, 0:wdt_],
                                        tq2[:, 0:wdt_], ALU.subtract))
                                    e.wait_ge(v_q2.h, v_q2.n)
                t_a.n = b[t_a] + 2048
                t_b.n = b[t_b] + 2048
                t_c.n = b[t_c] + 32
                s_a.n = b[s_a] + 2048
                v_a.n = b[v_a] + 32
                v_r.n = b[v_r] + 32
                v_b.n = b[v_b] + 32
                v_q.n = b[v_q] + NQ + 3
                v_q2.n = b[v_q2] + NQ
                t_q.n = b[t_q] + 2

            if debug:
                b = snap()
                with nc.Block() as blk:
                    @blk.sync
                    def _(e):
                        barrier(e, b)
                        e.dma_start(out=qdump[:], in_=qtsb[:]).then_inc(qs.h, 16)
                        e.dma_start(out=ndump[:, 0:16], in_=negc[:]) \
                            .then_inc(qs.h, 16)
                        e.wait_ge(qs.h, b[qs] + 32)
                qs.n += 32

        if upto >= 5 and not _SKIP_MID:
            with ExitStack() as ph:
                def psb(name, shape, dt):
                    return ph.enter_context(nc.sbuf_tensor(name, shape, dt, side="right"))
                wp = [psb(f"wp{i}", [128, NCK * 128], BF16) for i in range(3)]
                xm32 = [psb(f"xm32_{i}", [128, 512], F32) for i in range(3)]
                prs = [psb(f"prs{i}", [128, 512], F32) for i in range(3)]
                res1q = [psb(f"res1q{i}", [128, NCK * 512], F32)
                         for i in range(2)]
                sqt = [psb(f"sqt{i}", [128, 512], F32) for i in range(3)]
                m_all = psb("m_all", [1, 512], F32)
                r_all = psb("r_all", [1, 512], F32)
                tmsq = psb("tmsq", [1, 512], F32)
                tm2 = psb("tm2", [1, 512], F32)
                tm3 = psb("tm3", [1, 512], F32)
                tm4 = tm2
                rtmp = tm3
                ttmp = [psb("ttmp0", [128, 512], F32)] * 2
                y32 = [psb(f"y32_{i}", [128, 512], F32) for i in range(2)]
                pp = [ph.enter_context(nc.psum_tensor(f"pp{i}", [128, 512], F32))
                      for i in range(2)]
                pst_s = ph.enter_context(nc.psum_tensor("pst_s", [1, 512], F32))
                pst_q = ph.enter_context(nc.psum_tensor("pst_q", [1, 512], F32))
                pmb = [ph.enter_context(nc.psum_tensor(f"pmb{i}", [128, 512], F32))
                       for i in range(2)]
                prb = [ph.enter_context(nc.psum_tensor(f"prb{i}", [128, 512], F32))
                       for i in range(2)]
                r_wp = DmaRing(nc, es, sems, "r_wp", n=3, k=1)
                r_x32 = DmaRing(nc, es, sems, "r_x32", n=3, k=1)
                r_y32 = DmaRing(nc, es, sems, "r_y32", n=2, k=1)
                b = snap()

                def SA_evac(t):
                    return b[s_a] + (t // 16) * 17 + (t % 16) + 1

                def VA_add(t):
                    return b[v_a] + (t // 16) * 17 + (t % 16) + 1

                with nc.Block() as blk:
                    @blk.gpsimd
                    def _(e):
                        barrier(e, b)
                        for qt in range(4):
                            for dco in range(NCK):
                                t = qt * 16 + dco
                                if t >= 3:
                                    e.wait_ge(t_a.h, b[t_a] + t - 2)
                                r_wp.issue(
                                    e, t,
                                    out=wp[t % 3][:].rearrange("p (a o) -> p a o", a=NCK),
                                    in_=wproj[:, dco * 128:(dco + 1) * 128]
                                    .rearrange("(a p) o -> p a o", p=128))

                    @blk.sync
                    def _(e):
                        barrier(e, b)
                        def ystores(nq):
                            for dco in range(NCK):
                                t = nq * 16 + dco
                                e.wait_ge(v_b.h, b[v_b] + t + 1)
                                r_y32.issue(e, t,
                                            out=yscr[dco, :, nq * 512:(nq + 1) * 512],
                                            in_=y32[dco % 2][:])
                        for qt in range(4):
                            for dco in range(NCK):
                                t = qt * 16 + dco
                                if t >= 3:
                                    e.wait_ge(v_a.h, VA_add(t - 3))
                                r_x32.issue(e, t, out=xm32[t % 3][:],
                                            in_=xm[dco, :, qt * 512:(qt + 1) * 512])
                            if qt >= 1:
                                ystores(qt - 1)
                        ystores(3)

                    @blk.tensor
                    def _(e):
                        barrier(e, b)
                        def bcast(pq):
                            e.wait_ge(s_a.h, b[s_a] + (pq + 1) * 17)
                            e.matmul(pmb[pq % 2][:], ones_fr[0:1, :],
                                     m_all[0:1, :], start=True, stop=True)
                            mm = e.matmul(prb[pq % 2][:], ones_fr[0:1, :],
                                          r_all[0:1, :], start=True, stop=True)
                            t_c.inc(mm)
                        for qt in range(4):
                            for dco in range(NCK):
                                t = qt * 16 + dco
                                r_wp.wait(e, t)
                                if t >= 2:
                                    e.wait_ge(s_a.h, SA_evac(t - 2))
                                for dc in range(NCK):
                                    mm = e.matmul(
                                        pp[t % 2][:],
                                        wp[t % 3][:, dc * 128:(dc + 1) * 128],
                                        qtsb[:, dc * TOK + qt * 512:dc * TOK + (qt + 1) * 512],
                                        start=(dc == 0), stop=(dc == 15))
                                t_a.inc(mm)
                                if dco == 0 and qt >= 1:
                                    bcast(qt - 1)
                                if dco >= 2:
                                    pdc = dco - 2
                                    pt = qt * 16 + pdc
                                    e.wait_ge(v_a.h, VA_add(pt))
                                    e.matmul(pst_s[0:1, :], ones_fc[:, 0:1],
                                             res1q[qt % 2][:, pdc * 512:(pdc + 1) * 512],
                                             start=(pdc == 0), stop=False)
                                    mm = e.matmul(pst_q[0:1, :], ones_fc[:, 0:1],
                                                  sqt[pt % 3][:],
                                                  start=(pdc == 0), stop=False)
                                    t_b.inc(mm)
                            for pdc in (14, 15):
                                pt = qt * 16 + pdc
                                e.wait_ge(v_a.h, VA_add(pt))
                                e.matmul(pst_s[0:1, :], ones_fc[:, 0:1],
                                         res1q[qt % 2][:, pdc * 512:(pdc + 1) * 512],
                                         start=False, stop=(pdc == 15))
                                mm = e.matmul(pst_q[0:1, :], ones_fc[:, 0:1],
                                              sqt[pt % 3][:],
                                              start=False, stop=(pdc == 15))
                                t_b.inc(mm)
                        bcast(3)

                    @blk.scalar
                    def _(e):
                        barrier(e, b)
                        for qt in range(4):
                            for dco in range(NCK):
                                t = qt * 16 + dco
                                e.wait_ge(t_a.h, b[t_a] + t + 1)
                                if t >= 3:
                                    e.wait_ge(v_a.h, VA_add(t - 3))
                                s_a.inc(e.activation(prs[t % 3][:], pp[t % 2][:],
                                                     AF.Identity,
                                                     bias=bpc[:, dco:dco + 1]))
                            e.wait_ge(v_a.h, b[v_a] + qt * 17 + 17)
                            s_a.inc(e.activation(
                                r_all[0:1, :],
                                rtmp[0:1, :], AF.Sqrt))

                    @blk.vector
                    def _(e):
                        barrier(e, b)
                        def normalize(nq):
                            for dco in range(NCK):
                                t = nq * 16 + dco
                                if t >= 2:
                                    r_y32.wait(e, t - 2)
                                vsync(e, e.tensor_tensor(
                                    ttmp[dco % 2][:],
                                    res1q[nq % 2][:, dco * 512:(dco + 1) * 512],
                                    pmb[nq % 2][:], ALU.subtract))
                                v_b.inc(e.tensor_tensor(y32[dco % 2][:],
                                                        ttmp[dco % 2][:],
                                                        prb[nq % 2][:], ALU.mult))
                                e.wait_ge(v_b.h, v_b.n)
                        for qt in range(4):
                            for dco in range(NCK):
                                t = qt * 16 + dco
                                e.wait_ge(s_a.h, SA_evac(t))
                                r_x32.wait(e, t)
                                vsync(e, e.tensor_tensor(
                                    res1q[qt % 2][:, dco * 512:(dco + 1) * 512],
                                    prs[t % 3][:], xm32[t % 3][:], ALU.add))
                                v_a.inc(e.tensor_tensor(
                                    sqt[t % 3][:],
                                    res1q[qt % 2][:, dco * 512:(dco + 1) * 512],
                                    res1q[qt % 2][:, dco * 512:(dco + 1) * 512],
                                    ALU.mult))
                            e.wait_ge(t_b.h, b[t_b] + (qt + 1) * 16)
                            vsync(e, e.tensor_scalar(m_all[0:1, :],
                                                     pst_s[0:1, :], 1.0 / D,
                                                     None, ALU.mult))
                            vsync(e, e.tensor_scalar(tmsq[0:1, :], pst_q[0:1, :],
                                                     1.0 / D, None, ALU.mult))
                            vsync(e, e.tensor_tensor(
                                tm2[0:1, :],
                                m_all[0:1, :],
                                m_all[0:1, :], ALU.mult))
                            vsync(e, e.tensor_tensor(tm3[0:1, :], tmsq[0:1, :],
                                                     tm2[0:1, :], ALU.subtract))
                            vsync(e, e.tensor_scalar(tm4[0:1, :], tm3[0:1, :],
                                                     EPS_LN, None, ALU.add))
                            v_a.inc(e.reciprocal(rtmp[0:1, :], tm4[0:1, :]))
                            if qt >= 1:
                                e.wait_ge(t_c.h, b[t_c] + qt)
                                normalize(qt - 1)
                        e.wait_ge(t_c.h, b[t_c] + 4)
                        normalize(3)
                t_a.n = b[t_a] + 64
                t_b.n = b[t_b] + 64
                t_c.n = b[t_c] + 4
                s_a.n = b[s_a] + 68
                v_a.n = b[v_a] + 68
                v_b.n = b[v_b] + 64

            if debug:
                b = snap()
                with nc.Block() as blk:
                    @blk.sync
                    def _(e):
                        barrier(e, b)
                        e.dma_start(out=ydump[:], in_=ysb[:]).then_inc(qs.h, 16)
                        e.wait_ge(qs.h, b[qs] + 16)
                qs.n += 16
        qt_es.close()

        if upto >= 7:
            hidden = sb("hidden", [128, JCK * 1024], BF16)
            ysbh = sb("ysbh", [128, NCK * 1024], BF16)
            r_ysb = DmaRing(nc, es, sems, "r_ysb", n=1, k=16)
            r_w12 = DmaRing(nc, es, sems, "r_w12", n=2, k=32)
            r_w3 = DmaRing(nc, es, sems, "r_w3", n=4, k=1)
            r_yl = DmaRing(nc, es, sems, "r_yl", n=2, k=1)
            r_r2s = DmaRing(nc, es, sems, "r_r2s", n=2, k=1)
            r_rl = DmaRing(nc, es, sems, "r_rl", n=2, k=1)
            r_out = DmaRing(nc, es, sems, "r_out", n=2, k=1)
            for th in range(_FF_TH):
                toff = th * 1024
                th_es = ExitStack()
                m2s = th_es.enter_context(
                    nc.sbuf_tensor(f"m2s{th}", [1, 1024], F32, side="right"))
                r2s = th_es.enter_context(
                    nc.sbuf_tensor(f"r2s{th}", [1, 1024], F32, side="right"))
                with ExitStack() as ph:
                    def psb(name, shape, dt):
                        return ph.enter_context(nc.sbuf_tensor(name, shape, dt, side="right"))
                    w1t = [psb(f"w1t{th}_{i}", [128, NCK * 512], BF16)
                           for i in range(2)]
                    w2t = [psb(f"w2t{th}_{i}", [128, NCK * 512], BF16)
                           for i in range(2)]
                    s1s = [psb(f"s1s{th}_{i}", [128, 1024], BF16)
                           for i in range(2)]
                    px1 = [ph.enter_context(
                        nc.psum_tensor(f"px1{th}_{i}", [128, 1024], F32))
                        for i in range(2)]
                    px2 = [ph.enter_context(
                        nc.psum_tensor(f"px2{th}_{i}", [128, 1024], F32))
                        for i in range(2)]
                    b = snap()
                    with nc.Block() as blk:
                        @blk.gpsimd
                        def _(e):
                            barrier(e, b)
                            for dc in range(NCK):
                                r_ysb.issue(e, th,
                                            out=ysbh[:, dc * 1024:(dc + 1) * 1024],
                                            in_=yscr[dc, :, toff:toff + 1024])

                        @blk.sync
                        def _(e):
                            barrier(e, b)
                            for jg in range(11):
                                if jg >= 2:
                                    e.wait_ge(t_a.h, b[t_a] + (jg - 1) * 4)
                                for dc in range(NCK):
                                    r_w12.issue(
                                        e, th * 11 + jg,
                                        out=w1t[jg % 2][:, dc * 512:(dc + 1) * 512],
                                        in_=w1q[jg * 512:(jg + 1) * 512,
                                                dc * 128:(dc + 1) * 128],
                                        transpose=True)
                                for dc in range(NCK):
                                    r_w12.issue(
                                        e, th * 11 + jg,
                                        out=w2t[jg % 2][:, dc * 512:(dc + 1) * 512],
                                        in_=w2q[jg * 512:(jg + 1) * 512,
                                                dc * 128:(dc + 1) * 128],
                                        transpose=True)

                        @blk.tensor
                        def _(e):
                            barrier(e, b)
                            r_ysb.wait(e, th)
                            for jg in range(11):
                                r_w12.wait(e, th * 11 + jg)
                                for jc2 in range(4):
                                    jc = jg * 4 + jc2
                                    if jc >= 2:
                                        e.wait_ge(v_a.h, b[v_a] + jc - 1)
                                    for q2 in range(2):
                                        for dc in range(NCK):
                                            e.matmul(
                                                px1[jc % 2][:, q2 * 512:(q2 + 1) * 512],
                                                w1t[jg % 2][:, dc * 512 + jc2 * 128:
                                                             dc * 512 + jc2 * 128 + 128],
                                                ysbh[:, dc * 1024 + q2 * 512:
                                                     dc * 1024 + (q2 + 1) * 512],
                                                start=(dc == 0), stop=(dc == 15))
                                    for q2 in range(2):
                                        for dc in range(NCK):
                                            mm = e.matmul(
                                                px2[jc % 2][:, q2 * 512:(q2 + 1) * 512],
                                                w2t[jg % 2][:, dc * 512 + jc2 * 128:
                                                             dc * 512 + jc2 * 128 + 128],
                                                ysbh[:, dc * 1024 + q2 * 512:
                                                     dc * 1024 + (q2 + 1) * 512],
                                                start=(dc == 0), stop=(dc == 15))
                                    t_a.inc(mm)

                        @blk.scalar
                        def _(e):
                            barrier(e, b)
                            for jc in range(JCK):
                                e.wait_ge(t_a.h, b[t_a] + jc + 1)
                                if jc >= 2:
                                    e.wait_ge(v_a.h, b[v_a] + jc - 1)
                                s_a.inc(e.activation(s1s[jc % 2][:],
                                                     px1[jc % 2][:],
                                                     AF.Sigmoid))

                        @blk.vector
                        def _(e):
                            barrier(e, b)
                            for jc in range(JCK):
                                e.wait_ge(s_a.h, b[s_a] + jc + 1)
                                v_a.inc(e.tensor_tensor(
                                    hidden[:, jc * 1024:(jc + 1) * 1024],
                                    s1s[jc % 2][:], px2[jc % 2][:], ALU.mult))
                    t_a.n = b[t_a] + JCK
                    s_a.n = b[s_a] + JCK
                    v_a.n = b[v_a] + JCK

                if _FF_PARTS < 2:
                    th_es.close()
                    continue
                with ExitStack() as ph:
                    def psb(name, shape, dt):
                        return ph.enter_context(nc.sbuf_tensor(name, shape, dt, side="right"))
                    w3t = [psb(f"w3t{th}_{i}", [128, 256], BF16) for i in range(4)]
                    y32l = [psb(f"y32l{th}_{i}", [128, 512], F32) for i in range(2)]
                    r2sb = [psb(f"r2sb{th}_{i}", [128, 512], F32) for i in range(2)]
                    r2sq = [psb(f"r2sq{th}_{i}", [128, 512], F32) for i in range(2)]
                    tms = psb(f"tms{th}", [1, 512], F32)
                    tmm = psb(f"tmm{th}", [1, 512], F32)
                    tmn = psb(f"tmn{th}", [1, 512], F32)
                    tmo = psb(f"tmo{th}", [1, 512], F32)
                    rtm2 = psb(f"rtm2{th}", [1, 1024], F32)
                    px3 = [ph.enter_context(
                        nc.psum_tensor(f"px3{th}_{i}", [128, 1024], F32))
                        for i in range(2)]
                    pls = [ph.enter_context(
                        nc.psum_tensor(f"pls{th}_{i}", [1, 512], F32))
                        for i in range(2)]
                    plq = [ph.enter_context(
                        nc.psum_tensor(f"plq{th}_{i}", [1, 512], F32))
                        for i in range(2)]
                    b = snap()

                    with nc.Block() as blk:
                        @blk.scalar
                        def _(e):
                            barrier(e, b)
                            for q2 in range(2):
                                e.wait_ge(v_a.h, b[v_a] + 32 + q2 + 1)
                                s_a.inc(e.activation(
                                    r2s[0:1, q2 * 512:(q2 + 1) * 512],
                                    rtm2[0:1, q2 * 512:(q2 + 1) * 512], AF.Sqrt))

                        @blk.sync
                        def _(e):
                            barrier(e, b)
                            for dg in range(8):
                                for hc in range(44):
                                    n = dg * 44 + hc
                                    if n >= 4:
                                        e.wait_ge(t_b.h, b[t_b] + n - 3)
                                    r_w3.issue(e, th * 352 + n,
                                               out=w3t[n % 4][:],
                                               in_=w3q[dg * 256:(dg + 1) * 256,
                                                       hc * 128:(hc + 1) * 128],
                                               transpose=True)
                                for u4 in range(4):
                                    u = dg * 4 + u4
                                    dc = dg * 2 + u4 // 2
                                    q2 = u4 % 2
                                    if u >= 2:
                                        e.wait_ge(v_a.h, b[v_a] + u - 1)
                                    r_yl.issue(
                                        e, th * 32 + u,
                                        out=y32l[u % 2][:],
                                        in_=yscr[dc, :, toff + q2 * 512:
                                                 toff + (q2 + 1) * 512])
                                for u4 in range(4):
                                    u = dg * 4 + u4
                                    dc = dg * 2 + u4 // 2
                                    q2 = u4 % 2
                                    e.wait_ge(v_a.h, b[v_a] + u + 1)
                                    r_r2s.issue(
                                        e, th * 32 + u,
                                        out=r2scr[dc, :, toff + q2 * 512:
                                                  toff + (q2 + 1) * 512],
                                        in_=r2sb[u % 2][:])

                        @blk.tensor
                        def _(e):
                            barrier(e, b)
                            for dg in range(8):
                                for hc in range(44):
                                    n = dg * 44 + hc
                                    r_w3.wait(e, th * 352 + n)
                                    if hc == 0 and dg >= 1:
                                        e.wait_ge(v_a.h, b[v_a] + dg * 4)
                                    for dcw in range(2):
                                        for q2 in range(2):
                                            mm = e.matmul(
                                                px3[dcw][:, q2 * 512:(q2 + 1) * 512],
                                                w3t[n % 4][:, dcw * 128:(dcw + 1) * 128],
                                                hidden[:, hc * 1024 + q2 * 512:
                                                       hc * 1024 + (q2 + 1) * 512],
                                                start=(hc == 0), stop=(hc == 43))
                                    t_b.inc(mm)
                                for u4 in range(4):
                                    u = dg * 4 + u4
                                    dcw = u4 // 2
                                    q2 = u4 % 2
                                    e.wait_ge(v_a.h, b[v_a] + u + 1)
                                    e.matmul(pls[q2][0:1, :], ones_fc[:, 0:1],
                                             r2sb[u % 2][:],
                                             start=(dg == 0 and dcw == 0),
                                             stop=(dg == 7 and dcw == 1))
                                    mm = e.matmul(plq[q2][0:1, :], ones_fc[:, 0:1],
                                                  r2sq[u % 2][:],
                                                  start=(dg == 0 and dcw == 0),
                                                  stop=(dg == 7 and dcw == 1))
                                    t_c.inc(mm)

                        @blk.vector
                        def _(e):
                            barrier(e, b)
                            for dg in range(8):
                                for u4 in range(4):
                                    u = dg * 4 + u4
                                    dcw = u4 // 2
                                    q2 = u4 % 2
                                    e.wait_ge(t_b.h, b[t_b] + (dg + 1) * 44)
                                    r_yl.wait(e, th * 32 + u)
                                    if u >= 2:
                                        e.wait_ge(t_c.h, b[t_c] + u - 1)
                                        r_r2s.wait(e, th * 32 + u - 2)
                                    vsync(e, e.tensor_tensor(
                                        r2sb[u % 2][:],
                                        px3[dcw][:, q2 * 512:(q2 + 1) * 512],
                                        y32l[u % 2][:], ALU.add))
                                    v_a.inc(e.tensor_tensor(r2sq[u % 2][:],
                                                            r2sb[u % 2][:],
                                                            r2sb[u % 2][:],
                                                            ALU.mult))
                            for q2 in range(2):
                                e.wait_ge(t_c.h, b[t_c] + 32)
                                vsync(e, e.tensor_scalar(
                                    m2s[0:1, q2 * 512:(q2 + 1) * 512],
                                    pls[q2][0:1, :], 1.0 / D, None, ALU.mult))
                                vsync(e, e.tensor_scalar(tms[0:1, :],
                                                         plq[q2][0:1, :],
                                                         1.0 / D, None, ALU.mult))
                                vsync(e, e.tensor_tensor(
                                    tmm[0:1, :],
                                    m2s[0:1, q2 * 512:(q2 + 1) * 512],
                                    m2s[0:1, q2 * 512:(q2 + 1) * 512], ALU.mult))
                                vsync(e, e.tensor_tensor(tmn[0:1, :], tms[0:1, :],
                                                         tmm[0:1, :], ALU.subtract))
                                vsync(e, e.tensor_scalar(tmo[0:1, :], tmn[0:1, :],
                                                         EPS_LN, None, ALU.add))
                                v_a.inc(e.reciprocal(
                                    rtm2[0:1, q2 * 512:(q2 + 1) * 512],
                                    tmo[0:1, :]))

                    t_b.n = b[t_b] + 8 * 44
                    t_c.n = b[t_c] + 32
                    v_a.n = b[v_a] + 34
                    s_a.n = b[s_a] + 2

                if _FF_PARTS < 3:
                    th_es.close()
                    continue
                with ExitStack() as ph:
                    def psb(name, shape, dt):
                        return ph.enter_context(nc.sbuf_tensor(name, shape, dt, side="right"))
                    r2l = [psb(f"r2l{th}_{i}", [128, 512], F32) for i in range(2)]
                    ot1 = [psb(f"ot1{th}_{i}", [128, 512], F32) for i in range(2)]
                    ot2 = [psb(f"ot2{th}_{i}", [128, 512], F32) for i in range(2)]
                    pmb2 = ph.enter_context(
                        nc.psum_tensor(f"pmb2{th}", [128, 512], F32))
                    prb2 = ph.enter_context(
                        nc.psum_tensor(f"prb2{th}", [128, 512], F32))
                    b = snap()

                    with nc.Block() as blk:
                        @blk.sync
                        def _(e):
                            barrier(e, b)
                            tgt = out if _NORM_TARGET_OUT else yscr
                            for u in range(32):
                                q2, dc = u // 16, u % 16
                                if u >= 2:
                                    e.wait_ge(v_b.h, b[v_b] + u - 1)
                                r_rl.issue(e, th * 32 + u,
                                           out=r2l[u % 2][:],
                                           in_=r2scr[dc, :, toff + q2 * 512:
                                                     toff + (q2 + 1) * 512])
                                if u >= 2 and _NORM_STORES:
                                    up = u - 2
                                    pq2, pdc = up // 16, up % 16
                                    r_out.issue(e, th * 32 + up,
                                                out=tgt[pdc, :, toff + pq2 * 512:
                                                        toff + (pq2 + 1) * 512],
                                                in_=ot2[up % 2][:])
                            if _NORM_STORES:
                                for up in (30, 31):
                                    e.wait_ge(v_b.h, b[v_b] + up + 1)
                                    pq2, pdc = up // 16, up % 16
                                    r_out.issue(e, th * 32 + up,
                                                out=tgt[pdc, :, toff + pq2 * 512:
                                                        toff + (pq2 + 1) * 512],
                                                in_=ot2[up % 2][:])
                            if th == 1 and _NORM_STORES:
                                e.wait_ge(r_out.c[0].h, r_out.c[0].n)
                                e.wait_ge(r_out.c[1].h, r_out.c[1].n)

                        @blk.tensor
                        def _(e):
                            barrier(e, b)
                            for q2 in range(2):
                                if not _NORM_BCAST:
                                    t_c.inc(e.nop())
                                    continue
                                if q2 >= 1:
                                    e.wait_ge(v_b.h, b[v_b] + 16)
                                e.matmul(pmb2[:], ones_fr[0:1, :],
                                         m2s[0:1, q2 * 512:(q2 + 1) * 512],
                                         start=True, stop=True)
                                mm = e.matmul(prb2[:], ones_fr[0:1, :],
                                              r2s[0:1, q2 * 512:(q2 + 1) * 512],
                                              start=True, stop=True)
                                t_c.inc(mm)

                        @blk.vector
                        def _(e):
                            barrier(e, b)
                            for q2 in range(2):
                                for dc in range(NCK):
                                    u = q2 * 16 + dc
                                    e.wait_ge(t_c.h, b[t_c] + q2 + 1)
                                    r_rl.wait(e, th * 32 + u)
                                    if u >= 2 and _NORM_STORES:
                                        r_out.wait(e, th * 32 + u - 2)
                                    if _NORM_COMPUTE:
                                        vsync(e, e.tensor_tensor(ot1[u % 2][:],
                                                                 r2l[u % 2][:],
                                                                 pmb2[:], ALU.subtract))
                                        v_b.inc(e.tensor_tensor(ot2[u % 2][:],
                                                                ot1[u % 2][:],
                                                                prb2[:], ALU.mult))
                                    else:
                                        v_b.inc(e.tensor_copy(ot2[u % 2][:],
                                                              r2l[u % 2][:]))
                    t_c.n = b[t_c] + 2
                    v_b.n = b[v_b] + 32
                th_es.close()

    return nc


def kernel(L0_attn_x, L0_attn_w_qkv, L0_attn_b_qkv, L0_attn_w_proj,
           L0_attn_b_proj, L0_ffn_w1, L0_ffn_w2, L0_ffn_w3):
    x2d = np.ascontiguousarray(
        np.asarray(L0_attn_x, dtype=np.float32).reshape(2 * LK, D))
    wqkv = np.ascontiguousarray(np.asarray(L0_attn_w_qkv, dtype=np.float32))
    bqkv = np.asarray(L0_attn_b_qkv, dtype=np.float32)
    wproj = np.ascontiguousarray(np.asarray(L0_attn_w_proj, dtype=np.float32))
    bproj = np.asarray(L0_attn_b_proj, dtype=np.float32)
    w1 = np.ascontiguousarray(np.asarray(L0_ffn_w1, dtype=np.float32))
    w2 = np.ascontiguousarray(np.asarray(L0_ffn_w2, dtype=np.float32))
    w3 = np.ascontiguousarray(np.asarray(L0_ffn_w3, dtype=np.float32))

    bq_col = np.ascontiguousarray(bqkv.reshape(48, 128).T)
    bq_row = np.ascontiguousarray(bqkv.reshape(1, 3 * D))
    bp_col = np.ascontiguousarray(bproj.reshape(16, 128).T)

    nc = build()
    in_maps = []
    for c in range(8):
        be, qslot = c // 4, c % 4
        xbat = x2d[be * LK:(be + 1) * LK]
        xbc = np.ascontiguousarray(xbat.T).reshape(NCK, 128, LK)
        xmine = xbat[qslot * TOK:(qslot + 1) * TOK]
        xmc = np.ascontiguousarray(xmine.T).reshape(NCK, 128, TOK)
        in_maps.append({
            "xb": xbc, "xm": xmc, "wqkv": wqkv, "bq_col": bq_col,
            "bq_row": bq_row, "wproj": wproj, "bp_col": bp_col,
            "w1": w1, "w2": w2, "w3": w3,
        })
    res = run_bass_kernel_spmd(nc, in_maps, core_ids=list(range(8)))
    outp = np.empty((2 * LK, D), dtype=np.float32)
    for c in range(8):
        be, qslot = c // 4, c % 4
        o = np.asarray(res.results[c]["out"]).reshape(D, TOK)
        outp[be * LK + qslot * TOK: be * LK + (qslot + 1) * TOK] = o.T
    return outp.reshape(2, LK, D)
